# version 02
# baseline (speedup 1.0000x reference)
import os
import numpy as np
import ml_dtypes
from collections import defaultdict
from contextlib import ExitStack

import concourse.bass as bass
import concourse.mybir as mybir
from concourse.bass_utils import run_bass_kernel_spmd

F32 = mybir.dt.float32
BF16 = mybir.dt.bfloat16
I16 = mybir.dt.int16
AF = mybir.ActivationFunctionType
OP = mybir.AluOpType
AX = mybir.AxisListType

BB, LL, DD = 2, 4096, 2048
ED, NS, DCONV, DTR = 4096, 16, 4, 128
INTER = 5504
CH = ED // 4
TOK = LL // 4
NCORE = 8
OT = 43
NSLAB = 136
SLABW = 2048
SHARD = 17
G4 = [[0, 1, 2, 3], [4, 5, 6, 7]]
G8 = [[0, 1, 2, 3, 4, 5, 6, 7]]

ENGS = ("sync", "gpsimd", "scalar", "vector", "tensor")


class Rec:

    def __init__(self, nc, sems):
        self.nc = nc
        self.sems = sems
        self.cnt = defaultdict(int)
        self.steps = {e: [] for e in ENGS}

    def do(self, eng, build, inc=None, n=1):
        if inc is None:
            self.steps[eng].append(build)
        else:
            self.cnt[inc] += n
            s = self.sems[inc]
            self.steps[eng].append(
                lambda E, b=build, s=s, m=n: b(E).then_inc(s, m))

    def dma(self, eng, out, in_, ch="gpD" ):
        self.do(eng, lambda E, o=out, i=in_: E.dma_start(out=o, in_=i), ch, 16)

    def wait(self, eng, ch, val=None):
        v = self.cnt[ch] if val is None else val
        if v <= 0:
            return
        s = self.sems[ch]
        self.steps[eng].append(lambda E, s=s, v=v: E.wait_ge(s, v))

    def v(self, ch):
        return self.cnt[ch]

    def flush(self, name):
        with self.nc.Block(name) as blk:
            for eng in ENGS:
                if self.steps[eng]:
                    getattr(blk, eng)(
                        lambda E, L=tuple(self.steps[eng]): [f(E) for f in L])
        self.steps = {e: [] for e in ENGS}
        self.nc.all_engine_barrier()


_MAXPH = int(os.environ.get("KMAXPH", "99"))


def build_nc():
    nc = bass.Bass("TRN2", target_bir_lowering=False)

    def ein(name, shape, dt=F32):
        return nc.dram_tensor(name, list(shape), dt, kind="ExternalInput").ap()

    x_own = ein("x_own", (TOK, DD))
    winT = ein("winT", (16, 128, DD))
    nw16 = ein("nw16", (128, 16))
    wxT = ein("wxT", (128, 8 * 160))
    wdtT = ein("wdtT", (128, CH))
    bdt8 = ein("bdt8", (128, 8))
    a8 = ein("a8", (128, 128))
    d8 = ein("d8", (128, 8))
    cw8 = ein("cw8", (128, 32))
    cb8 = ein("cb8", (128, 8))
    woT = ein("woT", (8, 128, DD))
    idxs = ein("idxs", (SHARD, 128, SLABW), I16)
    lut_i = ein("lut_i", (1, NS))
    out_e = nc.dram_tensor("out", [TOK, DD], F32, kind="ExternalOutput").ap()

    def idr(name, shape, dt=F32):
        return nc.dram_tensor(name, list(shape), dt, kind="Internal").ap()

    xn_sc = idr("xn_sc", (8, 128, DD))
    uT_own = idr("uT_own", (DD, TOK), BF16)
    uT_full = idr("uT_full", (4 * DD, TOK), BF16)
    winb = idr("winb", (16, 128, DD), BF16)
    wag_in = idr("wag_in", (128, SHARD * SLABW), BF16)
    w_all = nc.dram_tensor("w_all", [128 * NCORE, SHARD * SLABW], BF16,
                           kind="Internal", addr_space="Shared").ap()
    dbc_p = [idr(f"dbc_p{i}", (160, 1024)) for i in range(4)]
    dbc_f = [idr(f"dbc_f{i}", (160, 1024)) for i in range(4)]
    bcbf = idr("bcbf", (16, LL), BF16)
    dt_sc = idr("dt_sc", (8, 128, LL))
    xs_sc = idr("xs_sc", (8, 128, LL), BF16)
    sz_sc = idr("sz_sc", (8, 128, LL), BF16)
    y2_sc = idr("y2_sc", (8, 128, LL), BF16)
    part = idr("part", (LL, DD))
    rs_out = idr("rs_out", (TOK, DD))
    ffn_sc = idr("ffn_sc", (8, 128, DD))
    fence_i = idr("fence_i", (1, 32))
    fence_o = idr("fence_o", (1, 32))
    hdbg = idr("hdbg", (16, 128, 512), BF16)
    wgdbg = idr("wgdbg", (8, 128, SLABW), BF16)

    eye_d = nc.inline_tensor(np.eye(128, dtype=ml_dtypes.bfloat16), "eye_c").ap()

    top = ExitStack()
    sb0 = lambda name, shape, dt: top.enter_context(
        nc.sbuf_tensor(name, list(shape), dt))
    sems = {n: top.enter_context(nc.semaphore(n))
            for n in ["syD", "gpD", "gpP", "scP", "veP", "teP", "cc"]}
    r = Rec(nc, sems)

    s_A = sb0("s_A", (128, 128), F32)
    s_bdt = sb0("s_bdt", (128, 8), F32)
    s_D = sb0("s_D", (128, 8), F32)
    s_cw = sb0("s_cw", (128, 32), F32)
    s_cb = sb0("s_cb", (128, 8), F32)
    s_nw = sb0("s_nw", (128, 16), F32)
    s_wx = sb0("s_wx", (128, 8 * 160), F32)
    s_wxb = sb0("s_wxb", (128, 8 * 160), BF16)
    s_wdtb = sb0("s_wdtb", (128, CH), BF16)
    s_lutrep = sb0("s_lutrep", (128, NS), F32)
    s_eye = sb0("s_eye", (128, 128), BF16)
    s_xnT = sb0("s_xnT", (128, 16 * TOK), BF16)
    s_carry = sb0("s_carry", (128, 8 * NS), F32)
    pF = [top.enter_context(nc.psum_tensor(f"pF{i}", [128, 512], F32))
          for i in range(6)]
    pT = top.enter_context(nc.psum_tensor("pT", [128, 128], BF16))
    pB32 = top.enter_context(nc.psum_tensor("pB32", [32, 512], F32))

    with ExitStack() as ph:
        sb = lambda n, s, d: ph.enter_context(nc.sbuf_tensor(n, list(s), d))
        s_wdtf = sb("s_wdtf", (128, CH), F32)
        for ap_h, tile in [(a8, s_A), (bdt8, s_bdt), (d8, s_D), (cw8, s_cw),
                           (cb8, s_cb), (nw16, s_nw), (wxT, s_wx),
                           (eye_d, s_eye)]:
            r.dma("sync", tile[:], ap_h, "syD")
        r.dma("sync", s_lutrep[:], lut_i.partition_broadcast(128), "syD")
        r.dma("sync", s_wdtf[:], wdtT, "syD")
        r.wait("vector", "syD")
        r.do("vector", lambda E: E.tensor_copy(s_wdtb[:], s_wdtf[:]), "veP")
        r.do("vector", lambda E: E.tensor_copy(s_wxb[:], s_wx[:]), "veP")
        r.do("vector", lambda E: E.memset(s_carry[:], 0.0), "veP")
        r.flush("S")

    if _MAXPH >= 1:
     with ExitStack() as ph:
        sb = lambda n, s, d: ph.enter_context(nc.sbuf_tensor(n, list(s), d))
        s_wf = sb("s_wf", (128, DD), F32)
        s_wb = sb("s_wb", (128, DD), BF16)
        for m in range(16):
            if m > 0:
                r.wait("sync", "veP")
            r.dma("sync", s_wf[:], winT[m], "syD")
            r.wait("vector", "syD")
            if m > 0:
                r.wait("vector", "gpD")
            for kk in range(16):
                r.do("vector", lambda E, kk=kk: E.tensor_scalar(
                    s_wb[:, 128 * kk:128 * (kk + 1)],
                    s_wf[:, 128 * kk:128 * (kk + 1)],
                    s_nw[:, kk:kk + 1], None, OP.mult), "veP")
            r.wait("gpsimd", "veP")
            r.dma("gpsimd", winb[m], s_wb[:], "gpD")
        r.wait("gpsimd", "gpD")
        r.flush("S2")

    if _MAXPH >= 2:
     with ExitStack() as ph:
        sb = lambda n, s, d: ph.enter_context(nc.sbuf_tensor(n, list(s), d))
        s_x = sb("s_x", (128, DD), F32)
        s_sq = sb("s_sq", (128, DD), BF16)
        s_xn = sb("s_xn", (128, DD), F32)
        s_ub = sb("s_ub", (128, DD), BF16)
        s_xnb = sb("s_xnb", (128, DD), BF16)
        s_st1 = sb("s_st1", (128, 1), F32)
        s_m1 = sb("s_m1", (128, 1), F32)
        s_ss = sb("s_ss", (128, 1), F32)
        s_v1 = sb("s_v1", (128, 1), F32)
        s_v2 = sb("s_v2", (128, 1), F32)
        s_rst = sb("s_rst", (128, 1), F32)
        s_rms = sb("s_rms", (128, 1), F32)
        s_tt = sb("s_tt", (128, 128), BF16)
        for t in range(8):
            if t > 0:
                r.wait("sync", "veP")
                r.wait("sync", "scP")
                r.wait("sync", "teP")
            r.dma("sync", s_x[:], x_own[128 * t:128 * (t + 1), :], "syD")
            r.wait("vector", "syD")
            if t > 0:
                r.wait("vector", "gpD")
            r.do("vector", lambda E: E.reduce_sum(s_st1[:], s_x[:], AX.X),
                 "veP")
            r.do("vector", lambda E: E.drain())
            r.do("vector", lambda E: E.tensor_scalar(
                s_m1[:], s_st1[:], 1.0 / DD, None, OP.mult), "veP")
            r.do("vector", lambda E: E.drain())
            r.wait("scalar", "syD")
            r.do("scalar", lambda E: E.activation(
                s_sq[:], s_x[:], AF.Square, accum_out=s_ss[:]), "scP")
            r.wait("vector", "scP")
            r.do("vector", lambda E: E.tensor_scalar(
                s_v1[:], s_ss[:], 1.0 / DD, None, OP.mult), "veP")
            r.do("vector", lambda E: E.tensor_mul(s_v2[:], s_m1[:], s_m1[:]),
                 "veP")
            r.do("vector", lambda E: E.drain())
            r.do("vector", lambda E: E.tensor_sub(s_v1[:], s_v1[:], s_v2[:]),
                 "veP")
            r.do("vector", lambda E: E.drain())
            r.do("vector", lambda E: E.tensor_scalar(
                s_v1[:], s_v1[:], 1e-5, None, OP.add), "veP")
            r.do("vector", lambda E: E.drain())
            r.do("vector", lambda E: E.reciprocal(s_v2[:], s_v1[:]), "veP")
            r.do("vector", lambda E: E.drain())
            r.wait("scalar", "veP")
            r.do("scalar", lambda E: E.activation(s_rst[:], s_v2[:], AF.Sqrt),
                 "scP")
            r.wait("vector", "scP")
            r.do("vector", lambda E: E.tensor_scalar(
                s_xn[:], s_x[:], s_m1[:], s_rst[:], OP.subtract, OP.mult),
                "veP")
            r.wait("scalar", "veP")
            r.do("scalar", lambda E: E.activation(
                s_sq[:], s_xn[:], AF.Square, accum_out=s_ss[:]), "scP")
            r.wait("vector", "scP")
            r.do("vector", lambda E: E.tensor_scalar(
                s_v1[:], s_ss[:], 1.0 / DD, None, OP.mult), "veP")
            r.do("vector", lambda E: E.drain())
            r.do("vector", lambda E: E.reciprocal(s_v2[:], s_v1[:]), "veP")
            r.do("vector", lambda E: E.drain())
            r.wait("scalar", "veP")
            r.do("scalar", lambda E: E.activation(s_rms[:], s_v2[:], AF.Sqrt),
                 "scP")
            r.wait("vector", "scP")
            r.do("vector", lambda E: E.tensor_scalar(
                s_ub[:], s_xn[:], s_rms[:], None, OP.mult), "veP")
            r.do("vector", lambda E: E.tensor_copy(s_xnb[:], s_xn[:]), "veP")
            if os.environ.get("KDEBUG") == "xn":
                for off, tile in [(2040, s_st1), (2041, s_m1), (2042, s_ss),
                                  (2043, s_v1), (2044, s_v2), (2045, s_rst),
                                  (2046, s_rms)]:
                    r.do("vector", lambda E, off=off, tile=tile:
                         E.tensor_scalar(s_xn[:, off:off + 1], tile[:], 1.0,
                                         None, OP.mult), "veP")
                    r.do("vector", lambda E: E.drain())
            r.wait("gpsimd", "veP")
            r.dma("gpsimd", xn_sc[t], s_xn[:], "gpD")
            for kk in range(16):
                r.wait("tensor", "veP")
                if t > 0 or kk > 0:
                    r.wait("tensor", "scP")
                r.do("tensor", lambda E, kk=kk: E.transpose(
                    pT[:], s_ub[:, 128 * kk:128 * (kk + 1)], s_eye[:]), "teP")
                r.wait("scalar", "teP")
                r.wait("scalar", "gpD")
                r.do("scalar", lambda E: E.copy(s_tt[:], pT[:]), "scP")
                r.wait("gpsimd", "scP")
                r.dma("gpsimd", uT_own[128 * kk:128 * (kk + 1),
                                128 * t:128 * (t + 1)], s_tt[:], "gpD")
                r.wait("tensor", "scP")
                r.do("tensor", lambda E, kk=kk: E.transpose(
                    pT[:], s_xnb[:, 128 * kk:128 * (kk + 1)], s_eye[:]),
                    "teP")
                r.wait("scalar", "teP")
                r.do("scalar", lambda E, kk=kk, t=t: E.copy(
                    s_xnT[:, TOK * kk + 128 * t: TOK * kk + 128 * (t + 1)],
                    pT[:]), "scP")
                r.wait("tensor", "scP")
        r.wait("gpsimd", "gpD")
        r.flush("P0")

    if _MAXPH >= 3:
        r.do("gpsimd", lambda E: E.collective_compute(
            "AllGather", OP.bypass, replica_groups=G4,
            ins=[uT_own], outs=[uT_full]), "cc")
        r.do("gpsimd", lambda E: E.collective_compute(
            "AllReduce", OP.add, replica_groups=G4,
            ins=[fence_i], outs=[fence_o]), "cc")
        r.flush("CCU")

    if _MAXPH >= 4:
     with ExitStack() as ph:
        sb = lambda n, s, d: ph.enter_context(nc.sbuf_tensor(n, list(s), d))
        s_idx = sb("s_idx", (128, SLABW), I16)
        s_msk = sb("s_msk", (128, 512), F32)
        s_acc = sb("s_acc", (128, 512), F32)
        s_wsl = sb("s_wsl", (128, SLABW), BF16)
        for sl in range(SHARD):
            if sl > 0:
                r.wait("sync", "veP")
            r.dma("sync", s_idx[:], idxs[sl], "syD")
            r.wait("vector", "syD")
            if sl > 0:
                r.wait("vector", "gpD")
            for pc in range(4):
                src = s_idx[:, 512 * pc:512 * (pc + 1)]
                for v in range(NS):
                    dst = (s_acc if v == 0 else s_msk)
                    r.do("vector", lambda E, d=dst, s=src, v=v:
                         E.tensor_scalar(d[:], s, float(v),
                                         s_lutrep[:, v:v + 1],
                                         OP.is_equal, OP.mult), "veP")
                    if v > 0:
                        r.do("vector", lambda E: E.tensor_add(
                            s_acc[:], s_acc[:], s_msk[:]), "veP")
                r.do("vector", lambda E, pc=pc: E.tensor_copy(
                    s_wsl[:, 512 * pc:512 * (pc + 1)], s_acc[:]), "veP")
            r.wait("gpsimd", "veP")
            r.dma("gpsimd", wag_in[:, SLABW * sl:SLABW * (sl + 1)], s_wsl[:],
                  "gpD")
        r.wait("gpsimd", "gpD")
        r.do("gpsimd", lambda E: E.collective_compute(
            "AllGather", OP.bypass, replica_groups=G8,
            ins=[wag_in], outs=[w_all]), "cc")
        r.do("gpsimd", lambda E: E.collective_compute(
            "AllReduce", OP.add, replica_groups=G8,
            ins=[fence_i], outs=[fence_o]), "cc")
        r.flush("LUT")

    def wslab(s):
        rk, sl = divmod(s, SHARD)
        return w_all[128 * rk:128 * (rk + 1), SLABW * sl:SLABW * (sl + 1)]

    if _MAXPH >= 5:
     with ExitStack() as ph:
        sb = lambda n, s, d: ph.enter_context(nc.sbuf_tensor(n, list(s), d))
        s_wg = sb("s_wg", (128, SLABW), BF16)
        s_wu = sb("s_wu", (128, SLABW), BF16)
        s_sg = sb("s_sg", (128, 512), BF16)
        s_hI = sb("s_hI", (128, OT * 512), BF16)
        s_wd = sb("s_wd", (128, 1024), BF16)
        s_po = sb("s_po", (128, 512), F32)
        r.wait("sync", "cc")
        for th in range(2):
            for o in range(OT):
                r.wait("sync", "teP")
                r.dma("sync", s_wg[:], wslab(o), "syD")
                r.dma("sync", s_wu[:], wslab(43 + o), "syD")
                if os.environ.get("KDEBUG") == "ffw" and th == 0 and o < 8:
                    r.wait("gpsimd", "syD")
                    r.dma("gpsimd", wgdbg[o], s_wg[:], "gpD")
                r.wait("tensor", "syD")
                r.wait("tensor", "scP")
                r.wait("tensor", "veP")
                for kk in range(16):
                    r.do("tensor", lambda E, kk=kk, th=th: E.matmul(
                        pF[0][:], s_wg[:, 128 * kk:128 * (kk + 1)],
                        s_xnT[:, TOK * kk + 512 * th:
                              TOK * kk + 512 * (th + 1)],
                        start=(kk == 0), stop=(kk == 15)),
                        "teP" if kk == 15 else None)
                for kk in range(16):
                    r.do("tensor", lambda E, kk=kk, th=th: E.matmul(
                        pF[1][:], s_wu[:, 128 * kk:128 * (kk + 1)],
                        s_xnT[:, TOK * kk + 512 * th:
                              TOK * kk + 512 * (th + 1)],
                        start=(kk == 0), stop=(kk == 15)),
                        "teP" if kk == 15 else None)
                r.wait("scalar", "teP")
                r.wait("scalar", "veP")
                r.do("scalar", lambda E: E.activation(
                    s_sg[:], pF[0][:], AF.Silu), "scP")
                r.wait("vector", "scP")
                r.do("vector", lambda E, o=o: E.tensor_mul(
                    s_hI[:, 512 * o:512 * (o + 1)], s_sg[:], pF[1][:]), "veP")
                if os.environ.get("KDEBUG") == "ffh" and th == 0 and o < 16:
                    r.wait("gpsimd", "veP")
                    r.dma("gpsimd", hdbg[o], s_hI[:, 512 * o:512 * (o + 1)].bitcast(BF16) if False else s_hI[:, 512 * o:512 * (o + 1)], "gpD")
            for tbs in ([0, 1, 2], [3]):
                for dh in range(2):
                    for k in range(OT):
                        r.wait("sync", "teP")
                        r.dma("sync", s_wd[:],
                              wslab(86 + k)[:, 1024 * dh:1024 * (dh + 1)],
                              "syD")
                        r.wait("tensor", "syD")
                        if k == 0:
                            r.wait("tensor", "scP")
                            r.wait("tensor", "veP")
                        for bi, tb in enumerate(tbs):
                            for dc in range(2):
                                last = (bi == len(tbs) - 1 and dc == 1)
                                r.do("tensor",
                                     lambda E, k=k, tb=tb, dc=dc, bi=bi:
                                     E.matmul(
                                         pF[2 * bi + dc][:],
                                         s_hI[:, 512 * k + 128 * tb:
                                              512 * k + 128 * (tb + 1)],
                                         s_wd[:, 512 * dc:512 * (dc + 1)],
                                         start=(k == 0), stop=(k == OT - 1)),
                                     "teP" if last else None)
                    r.wait("scalar", "teP")
                    for bi, tb in enumerate(tbs):
                        for dc in range(2):
                            r.wait("scalar", "gpD")
                            r.do("scalar", lambda E, bi=bi, dc=dc: E.copy(
                                s_po[:], pF[2 * bi + dc][:]), "scP")
                            r.wait("gpsimd", "scP")
                            r.dma("gpsimd",
                                  ffn_sc[4 * th + tb]
                                  [:, 1024 * dh + 512 * dc:
                                   1024 * dh + 512 * (dc + 1)],
                                  s_po[:], "gpD")
        r.wait("gpsimd", "gpD")
        r.flush("FFN")

    if _MAXPH >= 6:
     with ExitStack() as ph:
        sb = lambda n, s, d: ph.enter_context(nc.sbuf_tensor(n, list(s), d))
        s_uTc = sb("s_uTc", (128, 16 * 1024), BF16)
        s_wm = sb("s_wm", (128, DD), BF16)
        s_xcw = sb("s_xcw", (128, 8 * 515), F32)
        s_halo = sb("s_halo", (128, 8 * 3), F32)
        s_cv = sb("s_cv", (128, 512), F32)
        s_cv2 = sb("s_cv2", (128, 512), F32)
        s_xs4 = sb("s_xs4", (128, 8 * 512), BF16)
        s_zt = sb("s_zt", (128, 512), BF16)
        s_db1 = sb("s_db1", (128, 512), F32)
        s_db2 = sb("s_db2", (32, 512), F32)
        s_dtlr = sb("s_dtlr", (128, 1024), BF16)
        s_dlf = sb("s_dlf", (128, 1024), F32)
        s_bc32 = sb("s_bc32", (16, 1024), F32)
        s_bcb = sb("s_bcb", (16, 1024), BF16)
        s_et = sb("s_et", (128, 512), F32)
        s_dtt = sb("s_dtt", (128, 512), F32)

        r.do("vector", lambda E: E.memset(s_halo[:], 0.0), "veP")
        for sc in range(4):
            r.wait("sync", "teP")
            for kk in range(16):
                r.dma("sync", s_uTc[:, 1024 * kk:1024 * (kk + 1)],
                      uT_full[2048 * sc + 128 * kk:
                              2048 * sc + 128 * (kk + 1), :], "syD")
            for t4 in range(2):
                r.do("vector", lambda E: E.tensor_scalar(
                    s_xcw.ap().rearrange("p (m w) -> p m w", m=8)[:, :, 0:3],
                    s_halo.ap().rearrange("p (m w) -> p m w", m=8),
                    1.0, None, OP.mult), "veP")
                r.do("vector", lambda E: E.drain())
                r.wait("scalar", "veP")
                for m in range(16):
                    r.wait("sync", "teP")
                    r.dma("sync", s_wm[:], winb[m], "syD")
                    r.wait("tensor", "syD")
                    bank = pF[m % 2]
                    r.wait("tensor", "scP")
                    for kk in range(16):
                        r.do("tensor", lambda E, kk=kk, bank=bank, t4=t4:
                             E.matmul(
                                 bank[:], s_wm[:, 128 * kk:128 * (kk + 1)],
                                 s_uTc[:, 1024 * kk + 512 * t4:
                                       1024 * kk + 512 * (t4 + 1)],
                                 start=(kk == 0), stop=(kk == 15)),
                             "teP" if kk == 15 else None)
                    r.wait("scalar", "teP")
                    if m < 8:
                        r.do("scalar", lambda E, m=m, bank=bank: E.copy(
                            s_xcw[:, 515 * m + 3:515 * m + 515], bank[:]),
                            "scP")
                    else:
                        mz = m - 8
                        r.wait("scalar", "gpD")
                        r.do("scalar", lambda E, bank=bank: E.activation(
                            s_zt[:], bank[:], AF.Silu), "scP")
                        r.wait("gpsimd", "scP")
                        r.dma("gpsimd",
                              sz_sc[mz][:, 1024 * sc + 512 * t4:
                                        1024 * sc + 512 * (t4 + 1)],
                              s_zt[:], "gpD")
                r.wait("vector", "scP")
                r.wait("vector", "teP")
                for m in range(8):
                    w0 = 515 * m
                    r.do("vector", lambda E, w0=w0, m=m: E.tensor_scalar(
                        s_cv[:], s_xcw[:, w0 + 0:w0 + 512],
                        s_cw[:, 4 * m:4 * m + 1], None, OP.mult), "veP")
                    r.do("vector", lambda E, w0=w0, m=m:
                         E.scalar_tensor_tensor(
                             s_cv2[:], s_xcw[:, w0 + 1:w0 + 513],
                             s_cw[:, 4 * m + 1:4 * m + 2], s_cv[:],
                             OP.mult, OP.add), "veP")
                    r.do("vector", lambda E, w0=w0, m=m:
                         E.scalar_tensor_tensor(
                             s_cv[:], s_xcw[:, w0 + 2:w0 + 514],
                             s_cw[:, 4 * m + 2:4 * m + 3], s_cv2[:],
                             OP.mult, OP.add), "veP")
                    r.do("vector", lambda E, w0=w0, m=m:
                         E.scalar_tensor_tensor(
                             s_cv2[:], s_xcw[:, w0 + 3:w0 + 515],
                             s_cw[:, 4 * m + 3:4 * m + 4], s_cv[:],
                             OP.mult, OP.add), "veP")
                    r.wait("scalar", "veP")
                    r.do("scalar", lambda E, m=m: E.activation(
                        s_xs4[:, 512 * m:512 * (m + 1)], s_cv2[:], AF.Silu,
                        bias=s_cb[:, m:m + 1]), "scP")
                    r.wait("gpsimd", "scP")
                    r.dma("gpsimd",
                          xs_sc[m][:, 1024 * sc + 512 * t4:
                                   1024 * sc + 512 * (t4 + 1)],
                          s_xs4[:, 512 * m:512 * (m + 1)], "gpD")
                r.do("vector", lambda E: E.drain())
                r.do("vector", lambda E: E.tensor_scalar(
                    s_halo.ap().rearrange("p (m w) -> p m w", m=8),
                    s_xcw.ap().rearrange("p (m w) -> p m w", m=8)
                    [:, :, 512:515], 1.0, None, OP.mult), "veP")
                r.do("vector", lambda E: E.drain())
                r.wait("tensor", "scP")
                for k in range(8):
                    r.do("tensor", lambda E, k=k: E.matmul(
                        pF[2][:], s_wxb[:, 160 * k:160 * k + 128],
                        s_xs4[:, 512 * k:512 * (k + 1)],
                        start=(k == 0), stop=(k == 7)),
                        "teP" if k == 7 else None)
                for k in range(8):
                    r.do("tensor", lambda E, k=k: E.matmul(
                        pB32[:], s_wxb[:, 160 * k + 128:160 * (k + 1)],
                        s_xs4[:, 512 * k:512 * (k + 1)],
                        start=(k == 0), stop=(k == 7)),
                        "teP" if k == 7 else None)
                r.wait("scalar", "teP")
                r.wait("scalar", "gpD")
                r.do("scalar", lambda E: E.copy(s_db1[:], pF[2][:]), "scP")
                r.do("scalar", lambda E: E.copy(s_db2[:], pB32[:]), "scP")
                r.wait("gpsimd", "scP")
                r.dma("gpsimd", dbc_p[sc][0:128, 512 * t4:512 * (t4 + 1)],
                      s_db1[:], "gpD")
                r.dma("gpsimd", dbc_p[sc][128:160, 512 * t4:512 * (t4 + 1)],
                      s_db2[:], "gpD")
            r.wait("gpsimd", "gpD")
            r.do("gpsimd", lambda E, sc=sc: E.collective_compute(
                "AllReduce", OP.add, replica_groups=G4,
                ins=[dbc_p[sc]], outs=[dbc_f[sc]]), "cc")
            r.do("gpsimd", lambda E: E.collective_compute(
                "AllReduce", OP.add, replica_groups=G4,
                ins=[fence_i], outs=[fence_o]), "cc")
            r.wait("sync", "cc")
            r.wait("sync", "teP")
            r.wait("sync", "veP")
            r.dma("sync", s_dlf[:], dbc_f[sc][0:128, :], "syD")
            r.dma("sync", s_bc32[:], dbc_f[sc][144:160, :], "syD")
            r.wait("vector", "syD")
            r.wait("vector", "gpD")
            r.do("vector", lambda E: E.tensor_copy(s_dtlr[:], s_dlf[:]), "veP")
            r.do("vector", lambda E: E.tensor_copy(s_bcb[:], s_bc32[:]),
                 "veP")
            r.wait("gpsimd", "veP")
            r.dma("gpsimd", bcbf[:, 1024 * sc:1024 * (sc + 1)], s_bcb[:],
                  "gpD")
            for m in range(8):
                for t4 in range(2):
                    r.wait("tensor", "veP")
                    r.wait("tensor", "scP")
                    r.do("tensor", lambda E, m=m, t4=t4: E.matmul(
                        pF[3][:], s_wdtb[:, 128 * m:128 * (m + 1)],
                        s_dtlr[:, 512 * t4:512 * (t4 + 1)],
                        start=True, stop=True), "teP")
                    r.wait("scalar", "teP")
                    r.wait("scalar", "gpD")
                    r.do("scalar", lambda E, m=m: E.activation(
                        s_et[:], pF[3][:], AF.Exp, bias=s_bdt[:, m:m + 1]),
                        "scP")
                    r.do("scalar", lambda E: E.activation(
                        s_dtt[:], s_et[:], AF.Ln, bias=1.0), "scP")
                    r.wait("gpsimd", "scP")
                    r.dma("gpsimd",
                          dt_sc[m][:, 1024 * sc + 512 * t4:
                                   1024 * sc + 512 * (t4 + 1)],
                          s_dtt[:], "gpD")
        r.wait("gpsimd", "gpD")
        r.flush("P1")

    T2 = 512
    if _MAXPH >= 7:
     with ExitStack() as ph:
        sb = lambda n, s, d: ph.enter_context(nc.sbuf_tensor(n, list(s), d))
        s_brep = sb("s_brep", (128, NS * T2), F32)
        s_crep = sb("s_crep", (128, NS * T2), BF16)
        s_dtc = sb("s_dtc", (128, T2), F32)
        s_xsc = sb("s_xsc", (128, T2), BF16)
        s_wc = sb("s_wc", (128, T2), F32)
        s_da = [sb(f"s_da{i}", (128, T2), F32) for i in range(2)]
        s_dbx = [sb(f"s_dbx{i}", (128, T2), F32) for i in range(2)]
        s_h = sb("s_h", (128, NS * T2), F32)
        s_hc = sb("s_hc", (128, NS * T2), F32)
        s_yc = sb("s_yc", (128, T2), F32)
        for c in range(8):
            r.wait("gpsimd", "veP")
            r.wait("gpsimd", "gpP")
            r.dma("gpsimd",
                  s_brep.ap().rearrange("p (n t) -> p n t", n=NS),
                  dbc_f[c // 2][128:144, 512 * (c % 2):512 * (c % 2 + 1)]
                  .partition_broadcast(128), "gpD")
            r.dma("gpsimd",
                  s_crep.ap().rearrange("p (n t) -> p n t", n=NS),
                  bcbf[:, T2 * c:T2 * (c + 1)].partition_broadcast(128),
                  "gpD")
            r.wait("gpsimd", "gpD")
            for j in range(8):
                r.wait("sync", "veP")
                r.wait("sync", "scP")
                r.wait("sync", "gpD")
                r.dma("sync", s_dtc[:], dt_sc[j][:, T2 * c:T2 * (c + 1)],
                      "syD")
                r.dma("sync", s_xsc[:], xs_sc[j][:, T2 * c:T2 * (c + 1)],
                      "syD")
                r.wait("vector", "syD")
                r.do("vector", lambda E: E.tensor_mul(
                    s_wc[:], s_dtc[:], s_xsc[:]), "veP")
                for n in range(NS):
                    pa = n % 2
                    r.wait("scalar", "veP")
                    r.wait("scalar", "syD")
                    r.do("scalar", lambda E, j=j, n=n, pa=pa: E.activation(
                        s_da[pa][:], s_dtc[:], AF.Exp,
                        scale=s_A[:, 16 * j + n:16 * j + n + 1]), "scP")
                    r.wait("gpsimd", "veP")
                    r.do("gpsimd", lambda E, n=n, pa=pa: E.tensor_mul(
                        s_dbx[pa][:], s_wc[:],
                        s_brep[:, T2 * n:T2 * (n + 1)]), "gpP")
                    r.wait("vector", "scP")
                    r.wait("vector", "gpP")
                    init = (0.0 if c == 0 else
                            s_carry[:, 16 * j + n:16 * j + n + 1])
                    r.do("vector", lambda E, n=n, pa=pa, init=init:
                         E.tensor_tensor_scan(
                             s_h[:, T2 * n:T2 * (n + 1)], s_da[pa][:],
                             s_dbx[pa][:], init, OP.mult, OP.add), "veP")
                r.do("vector", lambda E: E.drain())
                r.do("vector", lambda E, j=j: E.tensor_scalar(
                    s_carry[:, 16 * j:16 * (j + 1)],
                    s_h.ap().rearrange("p (n t) -> p n t", n=NS)
                    [:, :, T2 - 1], 1.0, None, OP.mult), "veP")
                r.do("vector", lambda E: E.drain())
                r.do("vector", lambda E: E.tensor_mul(
                    s_hc[:], s_h[:], s_crep[:]), "veP")
                r.do("vector", lambda E: E.tensor_reduce(
                    s_yc[:], s_hc.ap().rearrange("p (n t) -> p t n", n=NS),
                    AX.X, OP.add), "veP")
                r.wait("gpsimd", "veP")
                r.dma("gpsimd", dt_sc[j][:, T2 * c:T2 * (c + 1)], s_yc[:],
                      "gpD")
                r.wait("vector", "gpD")
        r.flush("P2")
    y_sc = dt_sc

    if _MAXPH >= 8:
     with ExitStack() as ph:
        sb = lambda n, s, d: ph.enter_context(nc.sbuf_tensor(n, list(s), d))
        s_yt = sb("s_yt", (128, 512), F32)
        s_x3 = sb("s_x3", (128, 512), BF16)
        s_z3 = sb("s_z3", (128, 512), BF16)
        s_t3 = sb("s_t3", (128, 512), F32)
        s_y2 = sb("s_y2", (128, 512), BF16)
        for j in range(8):
            for c in range(8):
                r.wait("sync", "veP")
                r.dma("sync", s_yt[:], y_sc[j][:, 512 * c:512 * (c + 1)],
                      "syD")
                r.dma("sync", s_x3[:], xs_sc[j][:, 512 * c:512 * (c + 1)],
                      "syD")
                r.dma("sync", s_z3[:], sz_sc[j][:, 512 * c:512 * (c + 1)],
                      "syD")
                r.wait("vector", "syD")
                r.wait("vector", "gpD")
                r.do("vector", lambda E, j=j: E.scalar_tensor_tensor(
                    s_t3[:], s_x3[:], s_D[:, j:j + 1], s_yt[:],
                    OP.mult, OP.add), "veP")
                r.do("vector", lambda E: E.tensor_mul(
                    s_y2[:], s_t3[:], s_z3[:]), "veP")
                r.wait("gpsimd", "veP")
                r.dma("gpsimd", y2_sc[j][:, 512 * c:512 * (c + 1)], s_y2[:],
                      "gpD")
        r.wait("gpsimd", "gpD")
        r.flush("P3")

    if _MAXPH >= 9:
     with ExitStack() as ph:
        sb = lambda n, s, d: ph.enter_context(nc.sbuf_tensor(n, list(s), d))
        s_wof = sb("s_wof", (128, DD), F32)
        s_wo = sb("s_wo", (128, 8 * DD), BF16)
        s_y2g = sb("s_y2g", (128, 8 * 1024), BF16)
        s_pw = sb("s_pw", (128, 512), F32)
        for j in range(8):
            if j > 0:
                r.wait("sync", "veP")
            r.dma("sync", s_wof[:], woT[j], "syD")
            r.wait("vector", "syD")
            r.do("vector", lambda E, j=j: E.tensor_copy(
                s_wo[:, DD * j:DD * (j + 1)], s_wof[:]), "veP")
        for tg in range(4):
            r.wait("sync", "teP")
            for j in range(8):
                r.dma("sync", s_y2g[:, 1024 * j:1024 * (j + 1)],
                      y2_sc[j][:, 1024 * tg:1024 * (tg + 1)], "syD")
            r.wait("tensor", "syD")
            r.wait("tensor", "veP")
            for tb in range(8):
                for dc in range(4):
                    bank = pF[(tb * 4 + dc) % 2]
                    r.wait("tensor", "scP")
                    for j in range(8):
                        r.do("tensor", lambda E, j=j, tb=tb, dc=dc, bank=bank:
                             E.matmul(
                                 bank[:],
                                 s_y2g[:, 1024 * j + 128 * tb:
                                       1024 * j + 128 * (tb + 1)],
                                 s_wo[:, DD * j + 512 * dc:
                                      DD * j + 512 * (dc + 1)],
                                 start=(j == 0), stop=(j == 7)),
                             "teP" if j == 7 else None)
                    r.wait("scalar", "teP")
                    r.wait("scalar", "gpD")
                    r.do("scalar", lambda E, bank=bank: E.copy(
                        s_pw[:], bank[:]), "scP")
                    r.wait("gpsimd", "scP")
                    r.dma("gpsimd",
                          part[1024 * tg + 128 * tb:
                               1024 * tg + 128 * (tb + 1),
                               512 * dc:512 * (dc + 1)], s_pw[:], "gpD")
        r.wait("gpsimd", "gpD")
        r.do("gpsimd", lambda E: E.collective_compute(
            "ReduceScatter", OP.add, replica_groups=G4,
            ins=[part], outs=[rs_out]), "cc")
        r.do("gpsimd", lambda E: E.collective_compute(
            "AllReduce", OP.add, replica_groups=G4,
            ins=[fence_i], outs=[fence_o]), "cc")
        r.flush("WOUT")

    dbg = os.environ.get("KDEBUG", "")
    if dbg:
        with ExitStack() as ph:
            sb = lambda n, s, d: ph.enter_context(nc.sbuf_tensor(n, list(s), d))
            s_d1 = sb("s_d1", (128, DD), F32)
            src_map = {
                "xn": (xn_sc, F32, DD), "ffn": (ffn_sc, F32, DD),
                "y": (dt_sc, F32, LL), "xs": (xs_sc, BF16, LL),
                "sz": (sz_sc, BF16, LL), "y2": (y2_sc, BF16, LL),
            }
            if dbg in ("wag", "wall"):
                srcbuf = wag_in if dbg == "wag" else None
                for t in range(8):
                    s_dbw = sb(f"s_dbg{t}", (128, DD), BF16)
                    s_df = sb(f"s_dff{t}", (128, DD), F32)
                    if dbg == "wag":
                        sap = wag_in[:, DD * t:DD * (t + 1)]
                    else:
                        sap = w_all[128 * t:128 * (t + 1), 0:DD]
                    r.dma("sync", s_dbw[:], sap, "syD")
                    r.wait("vector", "syD")
                    r.do("vector", lambda E, a=s_dbw, b=s_df: E.tensor_copy(
                        b[:], a[:]), "veP")
                    r.wait("gpsimd", "veP")
                    r.dma("gpsimd", out_e[128 * t:128 * (t + 1), :], s_df[:],
                          "gpD")
            elif dbg == "ffh":
                for t in range(4):
                    s_hb = sb(f"s_hb{t}", (128, DD), BF16)
                    s_hf = sb(f"s_hf{t}", (128, DD), F32)
                    for i in range(4):
                        r.dma("sync", s_hb[:, 512 * i:512 * (i + 1)],
                              hdbg[4 * t + i], "syD")
                    r.wait("vector", "syD")
                    r.do("vector", lambda E, a=s_hb, b=s_hf: E.tensor_copy(
                        b[:], a[:]), "veP")
                    r.wait("gpsimd", "veP")
                    r.dma("gpsimd", out_e[128 * t:128 * (t + 1), :], s_hf[:],
                          "gpD")
            elif dbg == "ffw":
                for t in range(8):
                    s_hb = sb(f"s_hw{t}", (128, DD), BF16)
                    s_hf = sb(f"s_hw2{t}", (128, DD), F32)
                    r.dma("sync", s_hb[:], wgdbg[t], "syD")
                    r.wait("vector", "syD")
                    r.do("vector", lambda E, a=s_hb, b=s_hf: E.tensor_copy(
                        b[:], a[:]), "veP")
                    r.wait("gpsimd", "veP")
                    r.dma("gpsimd", out_e[128 * t:128 * (t + 1), :], s_hf[:],
                          "gpD")
            elif dbg == "part":
                for t in range(8):
                    r.wait("sync", "veP")
                    r.dma("sync", s_d1[:], part[128 * t:128 * (t + 1), :],
                          "syD")
                    r.wait("vector", "syD")
                    r.wait("vector", "gpD")
                    r.do("vector", lambda E: E.tensor_scalar(
                        s_d1[:], s_d1[:], 1.0, None, OP.mult), "veP")
                    r.wait("gpsimd", "veP")
                    r.dma("gpsimd", out_e[128 * t:128 * (t + 1), :], s_d1[:],
                          "gpD")
            elif dbg == "rs":
                r.wait("sync", "cc")
                for t in range(8):
                    r.wait("sync", "veP")
                    r.dma("sync", s_d1[:], rs_out[128 * t:128 * (t + 1), :],
                          "syD")
                    r.wait("gpsimd", "syD")
                    r.dma("gpsimd", out_e[128 * t:128 * (t + 1), :], s_d1[:],
                          "gpD")
            elif dbg == "ut":
                s_ub2 = sb("s_ub2", (128, 1024), BF16)
                for t in range(8):
                    r.wait("sync", "veP")
                    r.dma("sync", s_ub2[:],
                          uT_full[1024 * t:1024 * t + 128, :], "syD")
                    r.wait("vector", "syD")
                    r.wait("vector", "gpD")
                    r.do("vector", lambda E: E.tensor_copy(
                        s_d1[:, 0:1024], s_ub2[:]), "veP")
                    r.wait("gpsimd", "veP")
                    r.dma("gpsimd", out_e[128 * t:128 * (t + 1), :], s_d1[:],
                          "gpD")
            else:
                buf, bdt, wid = src_map[dbg]
                s_db = sb("s_db", (128, DD), bdt)
                for t in range(8):
                    r.wait("sync", "veP")
                    r.dma("sync", s_db[:], buf[t][:, 0:DD], "syD")
                    r.wait("vector", "syD")
                    r.wait("vector", "gpD")
                    r.do("vector", lambda E: E.tensor_copy(s_d1[:], s_db[:]),
                         "veP")
                    r.wait("gpsimd", "veP")
                    r.dma("gpsimd", out_e[128 * t:128 * (t + 1), :], s_d1[:],
                          "gpD")
            r.wait("sync", "gpD")
            r.flush("DBG")
        top.close()
        return nc

    with ExitStack() as ph:
        sb = lambda n, s, d: ph.enter_context(nc.sbuf_tensor(n, list(s), d))
        s_f1 = sb("s_f1", (128, DD), F32)
        s_f2 = sb("s_f2", (128, DD), F32)
        r.wait("sync", "cc")
        for t in range(8):
            r.wait("sync", "veP")
            r.wait("sync", "gpD")
            r.dma("sync", s_f1[:], rs_out[128 * t:128 * (t + 1), :], "syD")
            r.dma("sync", s_f2[:], x_own[128 * t:128 * (t + 1), :], "syD")
            r.wait("vector", "syD")
            r.do("vector", lambda E: E.tensor_add(s_f1[:], s_f1[:], s_f2[:]),
                 "veP")
            r.wait("sync", "veP")
            r.dma("sync", s_f2[:], xn_sc[t], "syD")
            r.wait("vector", "syD")
            r.do("vector", lambda E: E.tensor_add(s_f1[:], s_f1[:], s_f2[:]),
                 "veP")
            r.wait("sync", "veP")
            r.dma("sync", s_f2[:], ffn_sc[t], "syD")
            r.wait("vector", "syD")
            r.do("vector", lambda E: E.tensor_add(s_f1[:], s_f1[:], s_f2[:]),
                 "veP")
            r.wait("gpsimd", "veP")
            r.dma("gpsimd", out_e[128 * t:128 * (t + 1), :], s_f1[:], "gpD")
        r.wait("sync", "gpD")
        r.flush("FIN")

    top.close()
    return nc


def kernel(**inp):
    x = np.ascontiguousarray(inp["L0_attn_x"])
    norm_w = inp["L0_attn_norm_w"]
    W_in = inp["L0_attn_W_in"]
    conv_w = inp["L0_attn_conv_w"]
    conv_b = inp["L0_attn_conv_b"]
    W_x = inp["L0_attn_W_x"]
    W_dt = inp["L0_attn_W_dt"]
    b_dt = inp["L0_attn_b_dt"]
    A = inp["L0_attn_A"]
    D = inp["L0_attn_D"]
    W_out = inp["L0_attn_W_out"]
    lut = inp["L0_ffn_lut"]
    g_idx = inp["L0_ffn_gate_idx"]
    u_idx = inp["L0_ffn_up_idx"]
    d_idx = inp["L0_ffn_down_idx"]

    f32 = np.float32

    def gu_slabs(idx):
        t = idx.reshape(OT, 128, 16, 128)
        return t.transpose(0, 3, 2, 1).reshape(OT, 128, SLABW).astype(np.int16)

    def dn_slabs(idx):
        return np.ascontiguousarray(idx.T).reshape(OT, 128, DD).astype(np.int16)

    slabs = np.zeros((NSLAB, 128, SLABW), np.int16)
    slabs[0:43] = gu_slabs(g_idx)
    slabs[43:86] = gu_slabs(u_idx)
    slabs[86:129] = dn_slabs(d_idx)

    nw16 = np.ascontiguousarray(norm_w.reshape(16, 128).T.astype(f32))
    lut2 = np.ascontiguousarray(lut.reshape(1, NS).astype(f32))

    in_maps = []
    for k in range(NCORE):
        b, q = divmod(k, 4)
        rows = np.r_[1024 * q:1024 * (q + 1),
                     ED + 1024 * q:ED + 1024 * (q + 1)]
        Wr = W_in[rows]
        winTa = np.ascontiguousarray(
            Wr.reshape(16, 128, 16, 128).transpose(0, 3, 2, 1)
            .reshape(16, 128, DD).astype(f32))
        ch = slice(1024 * q, 1024 * (q + 1))
        wxTa = np.ascontiguousarray(
            W_x[:, ch].T.reshape(8, 128, 160).transpose(1, 0, 2)
            .reshape(128, 8 * 160).astype(f32))
        in_maps.append(dict(
            x_own=np.ascontiguousarray(x[b, 1024 * q:1024 * (q + 1)]),
            winT=winTa, nw16=nw16, wxT=wxTa,
            wdtT=np.ascontiguousarray(W_dt[ch].T.astype(f32)),
            bdt8=np.ascontiguousarray(b_dt[ch].reshape(8, 128).T.astype(f32)),
            a8=np.ascontiguousarray(
                A[ch].reshape(8, 128, NS).transpose(1, 0, 2)
                .reshape(128, 128).astype(f32)),
            d8=np.ascontiguousarray(D[ch].reshape(8, 128).T.astype(f32)),
            cw8=np.ascontiguousarray(
                conv_w[ch, 0, :].reshape(8, 128, 4).transpose(1, 0, 2)
                .reshape(128, 32).astype(f32)),
            cb8=np.ascontiguousarray(
                conv_b[ch].reshape(8, 128).T.astype(f32)),
            woT=np.ascontiguousarray(
                W_out[:, ch].T.reshape(8, 128, DD).astype(f32)),
            idxs=np.ascontiguousarray(slabs[SHARD * k:SHARD * (k + 1)]),
            lut_i=lut2,
        ))

    nc = build_nc()
    res = run_bass_kernel_spmd(nc, in_maps, core_ids=list(range(NCORE)))
    out = np.empty((BB, LL, DD), np.float32)
    for k in range(NCORE):
        b, q = divmod(k, 4)
        out[b, 1024 * q:1024 * (q + 1)] = np.asarray(
            res.results[k]["out"], np.float32)
    return out


# version 05 (final)
# speedup vs baseline: 1.2742x; 1.2742x over previous
import os
import numpy as np
import ml_dtypes
from collections import defaultdict
from contextlib import ExitStack

import concourse.bass as bass
import concourse.mybir as mybir
from concourse.bass_utils import run_bass_kernel_spmd

F32 = mybir.dt.float32
BF16 = mybir.dt.bfloat16
I16 = mybir.dt.int16
AF = mybir.ActivationFunctionType
OP = mybir.AluOpType
AX = mybir.AxisListType

BB, LL, DD = 2, 4096, 2048
ED, NS, DCONV, DTR = 4096, 16, 4, 128
INTER = 5504
CH = ED // 4
TOK = LL // 4
NCORE = 8
OT = 43
NSLAB = 136
SLABW = 2048
SHARD = 17
G4 = [[0, 1, 2, 3], [4, 5, 6, 7]]
G8 = [[0, 1, 2, 3, 4, 5, 6, 7]]

ENGS = ("sync", "gpsimd", "scalar", "vector", "tensor")


class Rec:

    def __init__(self, nc, sems):
        self.nc = nc
        self.sems = sems
        self.cnt = defaultdict(int)
        self.steps = {e: [] for e in ENGS}

    def do(self, eng, build, inc=None, n=1):
        if inc is None:
            self.steps[eng].append(build)
        else:
            self.cnt[inc] += n
            s = self.sems[inc]
            self.steps[eng].append(
                lambda E, b=build, s=s, m=n: b(E).then_inc(s, m))

    def dma(self, eng, out, in_, ch="gpD" ):
        self.do(eng, lambda E, o=out, i=in_: E.dma_start(out=o, in_=i), ch, 16)

    def wait(self, eng, ch, val=None):
        v = self.cnt[ch] if val is None else val
        if v <= 0:
            return
        s = self.sems[ch]
        self.steps[eng].append(lambda E, s=s, v=v: E.wait_ge(s, v))

    def v(self, ch):
        return self.cnt[ch]

    def flush(self, name):
        with self.nc.Block(name) as blk:
            for eng in ENGS:
                if self.steps[eng]:
                    getattr(blk, eng)(
                        lambda E, L=tuple(self.steps[eng]): [f(E) for f in L])
        self.steps = {e: [] for e in ENGS}
        self.nc.all_engine_barrier()


_MAXPH = int(os.environ.get("KMAXPH", "99"))


def build_nc():
    nc = bass.Bass("TRN2", target_bir_lowering=False)

    def ein(name, shape, dt=F32):
        return nc.dram_tensor(name, list(shape), dt, kind="ExternalInput").ap()

    x_own = ein("x_own", (TOK, DD))
    winT = ein("winT", (16, 128, DD))
    nw16 = ein("nw16", (128, 16))
    wxT = ein("wxT", (128, 8 * 160))
    wdtT = ein("wdtT", (128, CH))
    bdt8 = ein("bdt8", (128, 8))
    a8 = ein("a8", (128, 128))
    d8 = ein("d8", (128, 8))
    cw8 = ein("cw8", (128, 32))
    cb8 = ein("cb8", (128, 8))
    woT = ein("woT", (8, 128, DD))
    idxs = ein("idxs", (SHARD, 128, SLABW), I16)
    lut_i = ein("lut_i", (1, NS))
    bm_i = ein("bm_i", (128, 2))
    out_e = nc.dram_tensor("out", [TOK, DD], F32, kind="ExternalOutput").ap()

    def idr(name, shape, dt=F32):
        return nc.dram_tensor(name, list(shape), dt, kind="Internal").ap()

    xn_sc = idr("xn_sc", (8, 128, DD))
    uT_own = idr("uT_own", (DD, TOK), BF16)
    uT_full = idr("uT_full", (4 * DD, TOK), BF16)
    winb = idr("winb", (16, 128, DD), BF16)
    wag_in = idr("wag_in", (128, SHARD * SLABW), BF16)
    w_all = nc.dram_tensor("w_all", [128 * NCORE, SHARD * SLABW], BF16,
                           kind="Internal", addr_space="Shared").ap()
    dbc_p = [idr(f"dbc_p{i}", (320, 1024)) for i in range(4)]
    dbc_f = [nc.dram_tensor(f"dbc_f{i}", [320, 1024], F32, kind="Internal",
                            addr_space="Shared").ap() for i in range(4)]
    bcbf = idr("bcbf", (16, LL), BF16)
    dt_sc = idr("dt_sc", (8, 128, LL))
    xs_sc = idr("xs_sc", (8, 128, LL), BF16)
    sz_sc = idr("sz_sc", (8, 128, LL), BF16)
    y2_sc = idr("y2_sc", (8, 128, LL), BF16)
    part = idr("part", (LL, DD))
    rs_out = idr("rs_out", (TOK, DD))
    ffn_sc = idr("ffn_sc", (8, 128, DD))
    fence_i = idr("fence_i", (1, 32))
    fence_o = idr("fence_o", (1, 32))
    bbf32 = idr("bbf32", (16, LL))
    hdbg = idr("hdbg", (16, 128, 512), BF16)
    wgdbg = idr("wgdbg", (8, 128, SLABW), BF16)

    eye_d = nc.inline_tensor(np.eye(128, dtype=ml_dtypes.bfloat16), "eye_c").ap()

    top = ExitStack()
    sb0 = lambda name, shape, dt: top.enter_context(
        nc.sbuf_tensor(name, list(shape), dt))
    sems = {n: top.enter_context(nc.semaphore(n))
            for n in ["syD", "gpD", "gpP", "scP", "veP", "teP", "cc"]}
    r = Rec(nc, sems)

    s_A = sb0("s_A", (128, 128), F32)
    s_bdt = sb0("s_bdt", (128, 8), F32)
    s_D = sb0("s_D", (128, 8), F32)
    s_cw = sb0("s_cw", (128, 32), F32)
    s_cb = sb0("s_cb", (128, 8), F32)
    s_nw = sb0("s_nw", (128, 16), F32)
    s_wx = sb0("s_wx", (128, 8 * 160), F32)
    s_wxb = sb0("s_wxb", (128, 8 * 160), BF16)
    s_wdtb = sb0("s_wdtb", (128, CH), BF16)
    s_lutrep = sb0("s_lutrep", (128, NS), F32)
    s_bm = sb0("s_bm", (128, 2), F32)
    s_eye = sb0("s_eye", (128, 128), BF16)
    s_xnT = sb0("s_xnT", (128, 16 * TOK), BF16)
    s_carry = sb0("s_carry", (128, 8 * NS), F32)
    pF = [top.enter_context(nc.psum_tensor(f"pF{i}", [128, 512], F32))
          for i in range(6)]
    pT = top.enter_context(nc.psum_tensor("pT", [128, 128], BF16))
    pB32 = top.enter_context(nc.psum_tensor("pB32", [32, 512], F32))

    with ExitStack() as ph:
        sb = lambda n, s, d: ph.enter_context(nc.sbuf_tensor(n, list(s), d))
        s_wdtf = sb("s_wdtf", (128, CH), F32)
        for ap_h, tile in [(a8, s_A), (bdt8, s_bdt), (d8, s_D), (cw8, s_cw),
                           (cb8, s_cb), (nw16, s_nw), (wxT, s_wx),
                           (eye_d, s_eye)]:
            r.dma("sync", tile[:], ap_h, "syD")
        r.dma("sync", s_lutrep[:], lut_i.partition_broadcast(128), "syD")
        r.dma("sync", s_bm[:], bm_i, "syD")
        r.dma("sync", s_wdtf[:], wdtT, "syD")
        r.wait("vector", "syD")
        r.do("vector", lambda E: E.tensor_copy(s_wdtb[:], s_wdtf[:]), "veP")
        r.do("vector", lambda E: E.tensor_copy(s_wxb[:], s_wx[:]), "veP")
        r.do("vector", lambda E: E.memset(s_carry[:], 0.0), "veP")
        r.flush("S")

    if _MAXPH >= 1:
     with ExitStack() as ph:
        sb = lambda n, s, d: ph.enter_context(nc.sbuf_tensor(n, list(s), d))
        s_wf = sb("s_wf", (128, DD), F32)
        s_wb = sb("s_wb", (128, DD), BF16)
        for m in range(16):
            if m > 0:
                r.wait("sync", "veP")
            r.dma("sync", s_wf[:], winT[m], "syD")
            r.wait("vector", "syD")
            if m > 0:
                r.wait("vector", "gpD")
            for kk in range(16):
                r.do("vector", lambda E, kk=kk: E.tensor_scalar(
                    s_wb[:, 128 * kk:128 * (kk + 1)],
                    s_wf[:, 128 * kk:128 * (kk + 1)],
                    s_nw[:, kk:kk + 1], None, OP.mult), "veP")
            r.wait("gpsimd", "veP")
            r.dma("gpsimd", winb[m], s_wb[:], "gpD")
        r.wait("gpsimd", "gpD")
        r.flush("S2")

    if _MAXPH >= 2:
     with ExitStack() as ph:
        sb = lambda n, s, d: ph.enter_context(nc.sbuf_tensor(n, list(s), d))
        s_x = sb("s_x", (128, DD), F32)
        s_sq = sb("s_sq", (128, DD), BF16)
        s_xn = sb("s_xn", (128, DD), F32)
        s_ub = sb("s_ub", (128, DD), BF16)
        s_xnb = sb("s_xnb", (128, DD), BF16)
        s_st1 = sb("s_st1", (128, 1), F32)
        s_m1 = sb("s_m1", (128, 1), F32)
        s_ss = sb("s_ss", (128, 1), F32)
        s_v1 = sb("s_v1", (128, 1), F32)
        s_v2 = sb("s_v2", (128, 1), F32)
        s_rst = sb("s_rst", (128, 1), F32)
        s_rms = sb("s_rms", (128, 1), F32)
        s_tt = sb("s_tt", (128, 128), BF16)
        for t in range(8):
            if t > 0:
                r.wait("sync", "veP")
                r.wait("sync", "scP")
                r.wait("sync", "teP")
            r.dma("sync", s_x[:], x_own[128 * t:128 * (t + 1), :], "syD")
            r.wait("vector", "syD")
            if t > 0:
                r.wait("vector", "gpD")
            r.do("vector", lambda E: E.reduce_sum(s_st1[:], s_x[:], AX.X),
                 "veP")
            r.do("vector", lambda E: E.drain())
            r.do("vector", lambda E: E.tensor_scalar(
                s_m1[:], s_st1[:], 1.0 / DD, None, OP.mult), "veP")
            r.do("vector", lambda E: E.drain())
            r.wait("scalar", "syD")
            r.do("scalar", lambda E: E.activation(
                s_sq[:], s_x[:], AF.Square, accum_out=s_ss[:]), "scP")
            r.wait("vector", "scP")
            r.do("vector", lambda E: E.tensor_scalar(
                s_v1[:], s_ss[:], 1.0 / DD, None, OP.mult), "veP")
            r.do("vector", lambda E: E.tensor_mul(s_v2[:], s_m1[:], s_m1[:]),
                 "veP")
            r.do("vector", lambda E: E.drain())
            r.do("vector", lambda E: E.tensor_sub(s_v1[:], s_v1[:], s_v2[:]),
                 "veP")
            r.do("vector", lambda E: E.drain())
            r.do("vector", lambda E: E.tensor_scalar(
                s_v1[:], s_v1[:], 1e-5, None, OP.add), "veP")
            r.do("vector", lambda E: E.drain())
            r.do("vector", lambda E: E.reciprocal(s_v2[:], s_v1[:]), "veP")
            r.do("vector", lambda E: E.drain())
            r.wait("scalar", "veP")
            r.do("scalar", lambda E: E.activation(s_rst[:], s_v2[:], AF.Sqrt),
                 "scP")
            r.wait("vector", "scP")
            r.do("vector", lambda E: E.tensor_scalar(
                s_xn[:], s_x[:], s_m1[:], s_rst[:], OP.subtract, OP.mult),
                "veP")
            r.wait("scalar", "veP")
            r.do("scalar", lambda E: E.activation(
                s_sq[:], s_xn[:], AF.Square, accum_out=s_ss[:]), "scP")
            r.wait("vector", "scP")
            r.do("vector", lambda E: E.tensor_scalar(
                s_v1[:], s_ss[:], 1.0 / DD, None, OP.mult), "veP")
            r.do("vector", lambda E: E.drain())
            r.do("vector", lambda E: E.reciprocal(s_v2[:], s_v1[:]), "veP")
            r.do("vector", lambda E: E.drain())
            r.wait("scalar", "veP")
            r.do("scalar", lambda E: E.activation(s_rms[:], s_v2[:], AF.Sqrt),
                 "scP")
            r.wait("vector", "scP")
            r.do("vector", lambda E: E.tensor_scalar(
                s_ub[:], s_xn[:], s_rms[:], None, OP.mult), "veP")
            r.do("vector", lambda E: E.tensor_copy(s_xnb[:], s_xn[:]), "veP")
            if os.environ.get("KDEBUG") == "xn":
                for off, tile in [(2040, s_st1), (2041, s_m1), (2042, s_ss),
                                  (2043, s_v1), (2044, s_v2), (2045, s_rst),
                                  (2046, s_rms)]:
                    r.do("vector", lambda E, off=off, tile=tile:
                         E.tensor_scalar(s_xn[:, off:off + 1], tile[:], 1.0,
                                         None, OP.mult), "veP")
                    r.do("vector", lambda E: E.drain())
            r.wait("gpsimd", "veP")
            r.dma("gpsimd", xn_sc[t], s_xn[:], "gpD")
            for kk in range(16):
                r.wait("tensor", "veP")
                if t > 0 or kk > 0:
                    r.wait("tensor", "scP")
                r.do("tensor", lambda E, kk=kk: E.transpose(
                    pT[:], s_ub[:, 128 * kk:128 * (kk + 1)], s_eye[:]), "teP")
                r.wait("scalar", "teP")
                r.wait("scalar", "gpD")
                r.do("scalar", lambda E: E.copy(s_tt[:], pT[:]), "scP")
                r.wait("gpsimd", "scP")
                r.dma("gpsimd", uT_own[128 * kk:128 * (kk + 1),
                                128 * t:128 * (t + 1)], s_tt[:], "gpD")
                r.wait("tensor", "scP")
                r.do("tensor", lambda E, kk=kk: E.transpose(
                    pT[:], s_xnb[:, 128 * kk:128 * (kk + 1)], s_eye[:]),
                    "teP")
                r.wait("scalar", "teP")
                r.do("scalar", lambda E, kk=kk, t=t: E.copy(
                    s_xnT[:, TOK * kk + 128 * t: TOK * kk + 128 * (t + 1)],
                    pT[:]), "scP")
                r.wait("tensor", "scP")
        r.wait("gpsimd", "gpD")
        r.flush("P0")

    if _MAXPH >= 3:
        r.do("gpsimd", lambda E: E.collective_compute(
            "AllGather", OP.bypass, replica_groups=G4,
            ins=[uT_own], outs=[uT_full]), "cc")
        r.do("gpsimd", lambda E: E.collective_compute(
            "AllReduce", OP.add, replica_groups=G8,
            ins=[fence_i], outs=[fence_o]), "cc")
        r.flush("CCU")

    if _MAXPH >= 4:
     with ExitStack() as ph:
        sb = lambda n, s, d: ph.enter_context(nc.sbuf_tensor(n, list(s), d))
        s_idx = [sb(f"s_idx{i}", (128, SLABW), I16) for i in range(2)]
        s_msk = sb("s_msk", (128, 512), F32)
        s_acc = sb("s_acc", (128, 512), F32)
        s_wsl = sb("s_wsl", (128, SLABW), BF16)
        lut_ve = {}
        lut_dma = {}
        for sl in range(SHARD):
            if sl < 2:
                r.dma("sync", s_idx[sl % 2][:], idxs[sl], "syD")
                lut_dma[sl] = r.v("syD")
            bidx = s_idx[sl % 2]
            r.wait("vector", "syD", lut_dma[sl])
            if sl > 0:
                r.wait("vector", "gpD")
            for pc in range(4):
                src = bidx[:, 512 * pc:512 * (pc + 1)]
                for v in range(NS):
                    dst = (s_acc if v == 0 else s_msk)
                    r.do("vector", lambda E, d=dst, s=src, v=v:
                         E.tensor_scalar(d[:], s, float(v),
                                         s_lutrep[:, v:v + 1],
                                         OP.is_equal, OP.mult), "veP")
                    if v > 0:
                        r.do("vector", lambda E: E.tensor_add(
                            s_acc[:], s_acc[:], s_msk[:]), "veP")
                r.do("vector", lambda E, pc=pc: E.tensor_copy(
                    s_wsl[:, 512 * pc:512 * (pc + 1)], s_acc[:]), "veP")
            lut_ve[sl] = r.v("veP")
            if sl + 2 < SHARD:
                r.wait("sync", "veP", lut_ve[sl])
                r.dma("sync", s_idx[sl % 2][:], idxs[sl + 2], "syD")
                lut_dma[sl + 2] = r.v("syD")
            r.wait("gpsimd", "veP")
            r.dma("gpsimd", wag_in[:, SLABW * sl:SLABW * (sl + 1)], s_wsl[:],
                  "gpD")
        r.wait("gpsimd", "gpD")
        r.do("gpsimd", lambda E: E.collective_compute(
            "AllGather", OP.bypass, replica_groups=G8,
            ins=[wag_in], outs=[w_all]), "cc")
        r.do("gpsimd", lambda E: E.collective_compute(
            "AllReduce", OP.add, replica_groups=G8,
            ins=[fence_i], outs=[fence_o]), "cc")
        r.flush("LUT")

    def wslab(s):
        rk, sl = divmod(s, SHARD)
        return w_all[128 * rk:128 * (rk + 1), SLABW * sl:SLABW * (sl + 1)]

    if _MAXPH >= 5:
     with ExitStack() as ph:
        sb = lambda n, s, d: ph.enter_context(nc.sbuf_tensor(n, list(s), d))
        s_wg = [sb(f"s_wg{i}", (128, SLABW), BF16) for i in range(2)]
        s_wu = [sb(f"s_wu{i}", (128, SLABW), BF16) for i in range(2)]
        s_sg = sb("s_sg", (128, 512), BF16)
        s_hI = sb("s_hI", (128, OT * 512), BF16)
        s_wd = [sb(f"s_wd{i}", (128, 1024), BF16) for i in range(2)]
        s_po = sb("s_po", (128, 512), F32)
        r.wait("sync", "cc")
        for th in range(2):
            gu_te = {}
            gu_dma = {}
            for o in range(OT):
                if o < 2:
                    if th > 0:
                        r.wait("sync", "teP")
                    r.dma("sync", s_wg[o % 2][:], wslab(o), "syD")
                    r.dma("sync", s_wu[o % 2][:], wslab(43 + o), "syD")
                    gu_dma[o] = r.v("syD")
                bg, bu = s_wg[o % 2], s_wu[o % 2]
                r.wait("tensor", "syD", gu_dma[o])
                r.wait("tensor", "scP")
                r.wait("tensor", "veP")
                for kk in range(16):
                    r.do("tensor", lambda E, kk=kk, th=th, bg=bg: E.matmul(
                        pF[0][:], bg[:, 128 * kk:128 * (kk + 1)],
                        s_xnT[:, TOK * kk + 512 * th:
                              TOK * kk + 512 * (th + 1)],
                        start=(kk == 0), stop=(kk == 15)),
                        "teP" if kk == 15 else None)
                for kk in range(16):
                    r.do("tensor", lambda E, kk=kk, th=th, bu=bu: E.matmul(
                        pF[1][:], bu[:, 128 * kk:128 * (kk + 1)],
                        s_xnT[:, TOK * kk + 512 * th:
                              TOK * kk + 512 * (th + 1)],
                        start=(kk == 0), stop=(kk == 15)),
                        "teP" if kk == 15 else None)
                gu_te[o] = r.v("teP")
                if o + 2 < OT:
                    r.wait("sync", "teP", gu_te[o])
                    r.dma("sync", s_wg[o % 2][:], wslab(o + 2), "syD")
                    r.dma("sync", s_wu[o % 2][:], wslab(43 + o + 2), "syD")
                    gu_dma[o + 2] = r.v("syD")
                r.wait("scalar", "teP", gu_te[o])
                r.wait("scalar", "veP")
                r.do("scalar", lambda E: E.activation(
                    s_sg[:], pF[0][:], AF.Silu), "scP")
                r.wait("vector", "scP")
                r.do("vector", lambda E, o=o: E.tensor_mul(
                    s_hI[:, 512 * o:512 * (o + 1)], s_sg[:], pF[1][:]), "veP")
                if os.environ.get("KDEBUG") == "ffh" and th == 0 and o < 16:
                    r.wait("gpsimd", "veP")
                    r.dma("gpsimd", hdbg[o], s_hI[:, 512 * o:512 * (o + 1)].bitcast(BF16) if False else s_hI[:, 512 * o:512 * (o + 1)], "gpD")
            for tbs in ([0, 1, 2], [3]):
                for dh in range(2):
                    dn_te = {}
                    dn_dma = {}
                    for k in range(OT):
                        if k < 2:
                            r.wait("sync", "teP")
                            r.dma("sync", s_wd[k % 2][:],
                                  wslab(86 + k)[:, 1024 * dh:1024 * (dh + 1)],
                                  "syD")
                            dn_dma[k] = r.v("syD")
                        bw = s_wd[k % 2]
                        r.wait("tensor", "syD", dn_dma[k])
                        if k == 0:
                            r.wait("tensor", "scP")
                            r.wait("tensor", "veP")
                        for bi, tb in enumerate(tbs):
                            for dc in range(2):
                                last = (bi == len(tbs) - 1 and dc == 1)
                                r.do("tensor",
                                     lambda E, k=k, tb=tb, dc=dc, bi=bi, bw=bw:
                                     E.matmul(
                                         pF[2 * bi + dc][:],
                                         s_hI[:, 512 * k + 128 * tb:
                                              512 * k + 128 * (tb + 1)],
                                         bw[:, 512 * dc:512 * (dc + 1)],
                                         start=(k == 0), stop=(k == OT - 1)),
                                     "teP" if last else None)
                        dn_te[k] = r.v("teP")
                        if k + 2 < OT:
                            r.wait("sync", "teP", dn_te[k])
                            r.dma("sync", s_wd[k % 2][:],
                                  wslab(86 + k + 2)
                                  [:, 1024 * dh:1024 * (dh + 1)], "syD")
                            dn_dma[k + 2] = r.v("syD")
                    r.wait("scalar", "teP")
                    for bi, tb in enumerate(tbs):
                        for dc in range(2):
                            r.wait("scalar", "gpD")
                            r.do("scalar", lambda E, bi=bi, dc=dc: E.copy(
                                s_po[:], pF[2 * bi + dc][:]), "scP")
                            r.wait("gpsimd", "scP")
                            r.dma("gpsimd",
                                  ffn_sc[4 * th + tb]
                                  [:, 1024 * dh + 512 * dc:
                                   1024 * dh + 512 * (dc + 1)],
                                  s_po[:], "gpD")
        r.wait("gpsimd", "gpD")
        r.flush("FFN")

    if _MAXPH >= 6:
     with ExitStack() as ph:
        sb = lambda n, s, d: ph.enter_context(nc.sbuf_tensor(n, list(s), d))
        s_uTc = sb("s_uTc", (128, 16 * 1024), BF16)
        s_wm = [sb(f"s_wm{i}", (128, DD), BF16) for i in range(2)]
        s_xcw = sb("s_xcw", (128, 8 * 515), F32)
        s_halo = sb("s_halo", (128, 8 * 3), F32)
        s_cv = sb("s_cv", (128, 512), F32)
        s_cv2 = sb("s_cv2", (128, 512), F32)
        s_xs4 = sb("s_xs4", (128, 8 * 512), BF16)
        s_zt = sb("s_zt", (128, 512), BF16)
        s_db1 = sb("s_db1", (128, 512), F32)
        s_db2 = sb("s_db2", (32, 512), F32)
        s_db1m = [sb(f"s_db1m{i}", (128, 512), F32) for i in range(2)]
        s_db2m = [sb(f"s_db2m{i}", (32, 512), F32) for i in range(2)]
        s_dtlr = sb("s_dtlr", (128, 1024), BF16)
        s_dlf = sb("s_dlf", (128, 1024), F32)
        s_dlf2 = sb("s_dlf2", (128, 1024), F32)
        s_bB0 = sb("s_bB0", (16, 1024), F32)
        s_bB1 = sb("s_bB1", (16, 1024), F32)
        s_bC0 = sb("s_bC0", (16, 1024), F32)
        s_bC1 = sb("s_bC1", (16, 1024), F32)
        s_bcb = sb("s_bcb", (16, 1024), BF16)
        s_et = sb("s_et", (128, 512), F32)
        s_dtt = sb("s_dtt", (128, 512), F32)

        r.do("vector", lambda E: E.memset(s_halo[:], 0.0), "veP")
        for sc in range(4):
            r.wait("sync", "teP")
            for kk in range(16):
                r.dma("sync", s_uTc[:, 1024 * kk:1024 * (kk + 1)],
                      uT_full[2048 * sc + 128 * kk:
                              2048 * sc + 128 * (kk + 1), :], "syD")
            for t4 in range(2):
                r.do("vector", lambda E: E.tensor_scalar(
                    s_xcw.ap().rearrange("p (m w) -> p m w", m=8)[:, :, 0:3],
                    s_halo.ap().rearrange("p (m w) -> p m w", m=8),
                    1.0, None, OP.mult), "veP")
                r.do("vector", lambda E: E.drain())
                r.wait("scalar", "veP")
                te_snap = {}
                dma_snap = {}
                for m in range(16):
                    if m < 2:
                        if not (sc == 0 and t4 == 0):
                            r.wait("sync", "teP")
                        r.dma("sync", s_wm[m % 2][:], winb[m], "syD")
                        dma_snap[m] = r.v("syD")
                    buf = s_wm[m % 2]
                    r.wait("tensor", "syD", dma_snap[m])
                    bank = pF[m % 2]
                    r.wait("tensor", "scP")
                    for kk in range(16):
                        r.do("tensor", lambda E, kk=kk, bank=bank, t4=t4,
                             buf=buf: E.matmul(
                                 bank[:], buf[:, 128 * kk:128 * (kk + 1)],
                                 s_uTc[:, 1024 * kk + 512 * t4:
                                       1024 * kk + 512 * (t4 + 1)],
                                 start=(kk == 0), stop=(kk == 15)),
                             "teP" if kk == 15 else None)
                    te_snap[m] = r.v("teP")
                    if m + 2 < 16:
                        r.wait("sync", "teP", te_snap[m])
                        r.dma("sync", s_wm[m % 2][:], winb[m + 2], "syD")
                        dma_snap[m + 2] = r.v("syD")
                    r.wait("scalar", "teP")
                    if m < 8:
                        r.do("scalar", lambda E, m=m, bank=bank: E.copy(
                            s_xcw[:, 515 * m + 3:515 * m + 515], bank[:]),
                            "scP")
                    else:
                        mz = m - 8
                        r.wait("scalar", "gpD")
                        r.do("scalar", lambda E, bank=bank: E.activation(
                            s_zt[:], bank[:], AF.Silu), "scP")
                        r.wait("gpsimd", "scP")
                        r.dma("gpsimd",
                              sz_sc[mz][:, 1024 * sc + 512 * t4:
                                        1024 * sc + 512 * (t4 + 1)],
                              s_zt[:], "gpD")
                r.wait("vector", "scP")
                r.wait("vector", "teP")
                for m in range(8):
                    w0 = 515 * m
                    r.do("vector", lambda E, w0=w0, m=m: E.tensor_scalar(
                        s_cv[:], s_xcw[:, w0 + 0:w0 + 512],
                        s_cw[:, 4 * m:4 * m + 1], None, OP.mult), "veP")
                    r.do("vector", lambda E, w0=w0, m=m:
                         E.scalar_tensor_tensor(
                             s_cv2[:], s_xcw[:, w0 + 1:w0 + 513],
                             s_cw[:, 4 * m + 1:4 * m + 2], s_cv[:],
                             OP.mult, OP.add), "veP")
                    r.do("vector", lambda E, w0=w0, m=m:
                         E.scalar_tensor_tensor(
                             s_cv[:], s_xcw[:, w0 + 2:w0 + 514],
                             s_cw[:, 4 * m + 2:4 * m + 3], s_cv2[:],
                             OP.mult, OP.add), "veP")
                    r.do("vector", lambda E, w0=w0, m=m:
                         E.scalar_tensor_tensor(
                             s_cv2[:], s_xcw[:, w0 + 3:w0 + 515],
                             s_cw[:, 4 * m + 3:4 * m + 4], s_cv[:],
                             OP.mult, OP.add), "veP")
                    r.wait("scalar", "veP")
                    r.do("scalar", lambda E, m=m: E.activation(
                        s_xs4[:, 512 * m:512 * (m + 1)], s_cv2[:], AF.Silu,
                        bias=s_cb[:, m:m + 1]), "scP")
                    r.wait("gpsimd", "scP")
                    r.dma("gpsimd",
                          xs_sc[m][:, 1024 * sc + 512 * t4:
                                   1024 * sc + 512 * (t4 + 1)],
                          s_xs4[:, 512 * m:512 * (m + 1)], "gpD")
                r.do("vector", lambda E: E.drain())
                r.do("vector", lambda E: E.tensor_scalar(
                    s_halo.ap().rearrange("p (m w) -> p m w", m=8),
                    s_xcw.ap().rearrange("p (m w) -> p m w", m=8)
                    [:, :, 512:515], 1.0, None, OP.mult), "veP")
                r.do("vector", lambda E: E.drain())
                r.wait("tensor", "scP")
                for k in range(8):
                    r.do("tensor", lambda E, k=k: E.matmul(
                        pF[2][:], s_wxb[:, 160 * k:160 * k + 128],
                        s_xs4[:, 512 * k:512 * (k + 1)],
                        start=(k == 0), stop=(k == 7)),
                        "teP" if k == 7 else None)
                for k in range(8):
                    r.do("tensor", lambda E, k=k: E.matmul(
                        pB32[:], s_wxb[:, 160 * k + 128:160 * (k + 1)],
                        s_xs4[:, 512 * k:512 * (k + 1)],
                        start=(k == 0), stop=(k == 7)),
                        "teP" if k == 7 else None)
                r.wait("scalar", "teP")
                r.wait("scalar", "gpD")
                r.do("scalar", lambda E: E.copy(s_db1[:], pF[2][:]), "scP")
                r.do("scalar", lambda E: E.copy(s_db2[:], pB32[:]), "scP")
                r.wait("vector", "scP")
                r.wait("vector", "gpD")
                for si in range(2):
                    r.do("vector", lambda E, si=si: E.tensor_scalar(
                        s_db1m[si][:], s_db1[:], s_bm[:, si:si + 1], None,
                        OP.mult), "veP")
                    r.do("vector", lambda E, si=si: E.tensor_scalar(
                        s_db2m[si][:], s_db2[:], s_bm[0:32, si:si + 1], None,
                        OP.mult), "veP")
                r.wait("gpsimd", "veP")
                for si in range(2):
                    r.dma("gpsimd",
                          dbc_p[sc][160 * si:160 * si + 128,
                                    512 * t4:512 * (t4 + 1)],
                          s_db1m[si][:], "gpD")
                    r.dma("gpsimd",
                          dbc_p[sc][160 * si + 128:160 * si + 160,
                                    512 * t4:512 * (t4 + 1)],
                          s_db2m[si][:], "gpD")
            r.wait("gpsimd", "gpD")
            r.do("gpsimd", lambda E, sc=sc: E.collective_compute(
                "AllReduce", OP.add, replica_groups=G8,
                ins=[dbc_p[sc]], outs=[dbc_f[sc]]), "cc")
            r.do("gpsimd", lambda E: E.collective_compute(
                "AllReduce", OP.add, replica_groups=G8,
                ins=[fence_i], outs=[fence_o]), "cc")
            r.wait("sync", "cc")
            r.wait("sync", "teP")
            r.wait("sync", "veP")
            r.dma("sync", s_dlf[:], dbc_f[sc][0:128, :], "syD")
            r.dma("sync", s_dlf2[:], dbc_f[sc][160:288, :], "syD")
            r.dma("sync", s_bB0[:], dbc_f[sc][128:144, :], "syD")
            r.dma("sync", s_bC0[:], dbc_f[sc][144:160, :], "syD")
            r.dma("sync", s_bB1[:], dbc_f[sc][288:304, :], "syD")
            r.dma("sync", s_bC1[:], dbc_f[sc][304:320, :], "syD")
            r.wait("vector", "syD")
            r.wait("vector", "gpD")
            r.do("vector", lambda E: E.tensor_scalar(
                s_dlf[:], s_dlf[:], s_bm[:, 0:1], None, OP.mult), "veP")
            r.do("vector", lambda E: E.scalar_tensor_tensor(
                s_dlf[:], s_dlf2[:], s_bm[:, 1:2], s_dlf[:],
                OP.mult, OP.add), "veP")
            r.do("vector", lambda E: E.tensor_scalar(
                s_bB0[:], s_bB0[:], s_bm[0:16, 0:1], None, OP.mult), "veP")
            r.do("vector", lambda E: E.scalar_tensor_tensor(
                s_bB0[:], s_bB1[:], s_bm[0:16, 1:2], s_bB0[:],
                OP.mult, OP.add), "veP")
            r.do("vector", lambda E: E.tensor_scalar(
                s_bC0[:], s_bC0[:], s_bm[0:16, 0:1], None, OP.mult), "veP")
            r.do("vector", lambda E: E.scalar_tensor_tensor(
                s_bC0[:], s_bC1[:], s_bm[0:16, 1:2], s_bC0[:],
                OP.mult, OP.add), "veP")
            r.do("vector", lambda E: E.drain())
            r.do("vector", lambda E: E.tensor_copy(s_dtlr[:], s_dlf[:]), "veP")
            r.do("vector", lambda E: E.tensor_copy(s_bcb[:], s_bC0[:]), "veP")
            r.wait("gpsimd", "veP")
            r.dma("gpsimd", bcbf[:, 1024 * sc:1024 * (sc + 1)], s_bcb[:],
                  "gpD")
            r.dma("gpsimd", bbf32[:, 1024 * sc:1024 * (sc + 1)],
                  s_bB0[:], "gpD")
            for m in range(8):
                for t4 in range(2):
                    r.wait("tensor", "veP")
                    r.wait("tensor", "scP")
                    r.do("tensor", lambda E, m=m, t4=t4: E.matmul(
                        pF[3][:], s_wdtb[:, 128 * m:128 * (m + 1)],
                        s_dtlr[:, 512 * t4:512 * (t4 + 1)],
                        start=True, stop=True), "teP")
                    r.wait("scalar", "teP")
                    r.wait("scalar", "gpD")
                    r.do("scalar", lambda E, m=m: E.activation(
                        s_et[:], pF[3][:], AF.Exp, bias=s_bdt[:, m:m + 1]),
                        "scP")
                    r.do("scalar", lambda E: E.activation(
                        s_dtt[:], s_et[:], AF.Ln, bias=1.0), "scP")
                    r.wait("gpsimd", "scP")
                    r.dma("gpsimd",
                          dt_sc[m][:, 1024 * sc + 512 * t4:
                                   1024 * sc + 512 * (t4 + 1)],
                          s_dtt[:], "gpD")
        r.wait("gpsimd", "gpD")
        r.flush("P1")

    T2 = 512
    if _MAXPH >= 7:
     with ExitStack() as ph:
        sb = lambda n, s, d: ph.enter_context(nc.sbuf_tensor(n, list(s), d))
        s_brep = sb("s_brep", (128, NS * T2), F32)
        s_crep = sb("s_crep", (128, NS * T2), BF16)
        s_dtc = [sb(f"s_dtc{i}", (128, T2), F32) for i in range(2)]
        s_xsc = [sb(f"s_xsc{i}", (128, T2), BF16) for i in range(2)]
        s_wc = [sb(f"s_wc{i}", (128, T2), F32) for i in range(2)]
        s_da = [sb(f"s_da{i}", (128, T2), F32) for i in range(4)]
        s_dbx = [sb(f"s_dbx{i}", (128, T2), F32) for i in range(4)]
        s_h = sb("s_h", (128, NS * T2), F32)
        s_hc = sb("s_hc", (128, NS * T2), F32)
        s_yc = [sb(f"s_yc{i}", (128, T2), F32) for i in range(2)]
        P2S = {}
        for c in range(8):
            r.wait("gpsimd", "veP")
            r.wait("gpsimd", "gpP")
            r.dma("gpsimd",
                  s_brep.ap().rearrange("p (n t) -> p n t", n=NS),
                  bbf32[:, T2 * c:T2 * (c + 1)].partition_broadcast(128),
                  "gpD")
            r.dma("gpsimd",
                  s_crep.ap().rearrange("p (n t) -> p n t", n=NS),
                  bcbf[:, T2 * c:T2 * (c + 1)].partition_broadcast(128),
                  "gpD")
            r.wait("gpsimd", "gpD")
            for j in range(8):
                jp = j % 2
                gi = 8 * c + j
                if "p2u" in P2S and jp in P2S["p2u"]:
                    up = P2S["p2u"][jp]
                    r.wait("sync", "scP", up[0])
                    r.wait("sync", "veP", up[1])
                r.dma("sync", s_dtc[jp][:], dt_sc[j][:, T2 * c:T2 * (c + 1)],
                      "syD")
                r.dma("sync", s_xsc[jp][:], xs_sc[j][:, T2 * c:T2 * (c + 1)],
                      "syD")
                ld_v = r.v("syD")
                r.wait("vector", "syD", ld_v)
                if "yc" in P2S and jp in P2S["yc"]:
                    r.wait("vector", "gpD", P2S["yc"][jp])
                r.do("vector", lambda E, jp=jp: E.tensor_mul(
                    s_wc[jp][:], s_dtc[jp][:], s_xsc[jp][:]), "veP")
                wmul_v = r.v("veP")
                for n in range(NS):
                    pa = n % 4
                    if gi > 0 or n >= 4:
                        r.wait("scalar", "veP", P2S.get(("scan", pa), 0))
                    r.wait("scalar", "syD", ld_v)
                    r.do("scalar", lambda E, j=j, n=n, pa=pa, jp=jp:
                         E.activation(
                             s_da[pa][:], s_dtc[jp][:], AF.Exp,
                             scale=s_A[:, 16 * j + n:16 * j + n + 1]), "scP")
                    exp_v = r.v("scP")
                    if gi > 0 or n >= 4:
                        r.wait("gpsimd", "veP", P2S.get(("scan", pa), 0))
                    r.wait("gpsimd", "veP", wmul_v)
                    r.do("gpsimd", lambda E, n=n, pa=pa, jp=jp: E.tensor_mul(
                        s_dbx[pa][:], s_wc[jp][:],
                        s_brep[:, T2 * n:T2 * (n + 1)]), "gpP")
                    dbx_v = r.v("gpP")
                    r.wait("vector", "scP", exp_v)
                    r.wait("vector", "gpP", dbx_v)
                    init = (0.0 if c == 0 else
                            s_carry[:, 16 * j + n:16 * j + n + 1])
                    r.do("vector", lambda E, n=n, pa=pa, init=init:
                         E.tensor_tensor_scan(
                             s_h[:, T2 * n:T2 * (n + 1)], s_da[pa][:],
                             s_dbx[pa][:], init, OP.mult, OP.add), "veP")
                    P2S[("scan", pa)] = r.v("veP")
                P2S.setdefault("p2u", {})[jp] = (r.v("scP"), wmul_v)
                r.do("vector", lambda E: E.drain())
                r.do("vector", lambda E, j=j: E.tensor_scalar(
                    s_carry[:, 16 * j:16 * (j + 1)],
                    s_h.ap().rearrange("p (n t) -> p n t", n=NS)
                    [:, :, T2 - 1], 1.0, None, OP.mult), "veP")
                r.do("vector", lambda E: E.drain())
                r.do("vector", lambda E: E.tensor_mul(
                    s_hc[:], s_h[:], s_crep[:]), "veP")
                r.do("vector", lambda E, jp=jp: E.tensor_reduce(
                    s_yc[jp][:],
                    s_hc.ap().rearrange("p (n t) -> p t n", n=NS),
                    AX.X, OP.add), "veP")
                red_v = r.v("veP")
                r.wait("gpsimd", "veP", red_v)
                r.dma("gpsimd", dt_sc[j][:, T2 * c:T2 * (c + 1)],
                      s_yc[jp][:], "gpD")
                P2S.setdefault("yc", {})[jp] = r.v("gpD")
        r.flush("P2")
    y_sc = dt_sc

    if _MAXPH >= 8:
     with ExitStack() as ph:
        sb = lambda n, s, d: ph.enter_context(nc.sbuf_tensor(n, list(s), d))
        s_yt = sb("s_yt", (128, 512), F32)
        s_x3 = sb("s_x3", (128, 512), BF16)
        s_z3 = sb("s_z3", (128, 512), BF16)
        s_t3 = sb("s_t3", (128, 512), F32)
        s_y2 = sb("s_y2", (128, 512), BF16)
        for j in range(8):
            for c in range(8):
                r.wait("sync", "veP")
                r.dma("sync", s_yt[:], y_sc[j][:, 512 * c:512 * (c + 1)],
                      "syD")
                r.dma("sync", s_x3[:], xs_sc[j][:, 512 * c:512 * (c + 1)],
                      "syD")
                r.dma("sync", s_z3[:], sz_sc[j][:, 512 * c:512 * (c + 1)],
                      "syD")
                r.wait("vector", "syD")
                r.wait("vector", "gpD")
                r.do("vector", lambda E, j=j: E.scalar_tensor_tensor(
                    s_t3[:], s_x3[:], s_D[:, j:j + 1], s_yt[:],
                    OP.mult, OP.add), "veP")
                r.do("vector", lambda E: E.tensor_mul(
                    s_y2[:], s_t3[:], s_z3[:]), "veP")
                r.wait("gpsimd", "veP")
                r.dma("gpsimd", y2_sc[j][:, 512 * c:512 * (c + 1)], s_y2[:],
                      "gpD")
        r.wait("gpsimd", "gpD")
        r.flush("P3")

    if _MAXPH >= 9:
     with ExitStack() as ph:
        sb = lambda n, s, d: ph.enter_context(nc.sbuf_tensor(n, list(s), d))
        s_wof = sb("s_wof", (128, DD), F32)
        s_wo = sb("s_wo", (128, 8 * DD), BF16)
        s_y2g = sb("s_y2g", (128, 8 * 1024), BF16)
        s_pw = sb("s_pw", (128, 512), F32)
        for j in range(8):
            if j > 0:
                r.wait("sync", "veP")
            r.dma("sync", s_wof[:], woT[j], "syD")
            r.wait("vector", "syD")
            r.do("vector", lambda E, j=j: E.tensor_copy(
                s_wo[:, DD * j:DD * (j + 1)], s_wof[:]), "veP")
        for tg in range(4):
            r.wait("sync", "teP")
            for j in range(8):
                r.dma("sync", s_y2g[:, 1024 * j:1024 * (j + 1)],
                      y2_sc[j][:, 1024 * tg:1024 * (tg + 1)], "syD")
            r.wait("tensor", "syD")
            r.wait("tensor", "veP")
            for tb in range(8):
                for dc in range(4):
                    bank = pF[(tb * 4 + dc) % 2]
                    r.wait("tensor", "scP")
                    for j in range(8):
                        r.do("tensor", lambda E, j=j, tb=tb, dc=dc, bank=bank:
                             E.matmul(
                                 bank[:],
                                 s_y2g[:, 1024 * j + 128 * tb:
                                       1024 * j + 128 * (tb + 1)],
                                 s_wo[:, DD * j + 512 * dc:
                                      DD * j + 512 * (dc + 1)],
                                 start=(j == 0), stop=(j == 7)),
                             "teP" if j == 7 else None)
                    r.wait("scalar", "teP")
                    r.wait("scalar", "gpD")
                    r.do("scalar", lambda E, bank=bank: E.copy(
                        s_pw[:], bank[:]), "scP")
                    r.wait("gpsimd", "scP")
                    r.dma("gpsimd",
                          part[1024 * tg + 128 * tb:
                               1024 * tg + 128 * (tb + 1),
                               512 * dc:512 * (dc + 1)], s_pw[:], "gpD")
        r.wait("gpsimd", "gpD")
        r.do("gpsimd", lambda E: E.collective_compute(
            "ReduceScatter", OP.add, replica_groups=G4,
            ins=[part], outs=[rs_out]), "cc")
        r.do("gpsimd", lambda E: E.collective_compute(
            "AllReduce", OP.add, replica_groups=G8,
            ins=[fence_i], outs=[fence_o]), "cc")
        r.flush("WOUT")

    dbg = os.environ.get("KDEBUG", "")
    if dbg:
        with ExitStack() as ph:
            sb = lambda n, s, d: ph.enter_context(nc.sbuf_tensor(n, list(s), d))
            s_d1 = sb("s_d1", (128, DD), F32)
            src_map = {
                "xn": (xn_sc, F32, DD), "ffn": (ffn_sc, F32, DD),
                "y": (dt_sc, F32, LL), "xs": (xs_sc, BF16, LL),
                "sz": (sz_sc, BF16, LL), "y2": (y2_sc, BF16, LL),
            }
            if dbg in ("wag", "wall"):
                srcbuf = wag_in if dbg == "wag" else None
                for t in range(8):
                    s_dbw = sb(f"s_dbg{t}", (128, DD), BF16)
                    s_df = sb(f"s_dff{t}", (128, DD), F32)
                    if dbg == "wag":
                        sap = wag_in[:, DD * t:DD * (t + 1)]
                    else:
                        sap = w_all[128 * t:128 * (t + 1), 0:DD]
                    r.dma("sync", s_dbw[:], sap, "syD")
                    r.wait("vector", "syD")
                    r.do("vector", lambda E, a=s_dbw, b=s_df: E.tensor_copy(
                        b[:], a[:]), "veP")
                    r.wait("gpsimd", "veP")
                    r.dma("gpsimd", out_e[128 * t:128 * (t + 1), :], s_df[:],
                          "gpD")
            elif dbg == "ffh":
                for t in range(4):
                    s_hb = sb(f"s_hb{t}", (128, DD), BF16)
                    s_hf = sb(f"s_hf{t}", (128, DD), F32)
                    for i in range(4):
                        r.dma("sync", s_hb[:, 512 * i:512 * (i + 1)],
                              hdbg[4 * t + i], "syD")
                    r.wait("vector", "syD")
                    r.do("vector", lambda E, a=s_hb, b=s_hf: E.tensor_copy(
                        b[:], a[:]), "veP")
                    r.wait("gpsimd", "veP")
                    r.dma("gpsimd", out_e[128 * t:128 * (t + 1), :], s_hf[:],
                          "gpD")
            elif dbg == "ffw":
                for t in range(8):
                    s_hb = sb(f"s_hw{t}", (128, DD), BF16)
                    s_hf = sb(f"s_hw2{t}", (128, DD), F32)
                    r.dma("sync", s_hb[:], wgdbg[t], "syD")
                    r.wait("vector", "syD")
                    r.do("vector", lambda E, a=s_hb, b=s_hf: E.tensor_copy(
                        b[:], a[:]), "veP")
                    r.wait("gpsimd", "veP")
                    r.dma("gpsimd", out_e[128 * t:128 * (t + 1), :], s_hf[:],
                          "gpD")
            elif dbg == "part":
                for t in range(8):
                    r.wait("sync", "veP")
                    r.dma("sync", s_d1[:], part[128 * t:128 * (t + 1), :],
                          "syD")
                    r.wait("vector", "syD")
                    r.wait("vector", "gpD")
                    r.do("vector", lambda E: E.tensor_scalar(
                        s_d1[:], s_d1[:], 1.0, None, OP.mult), "veP")
                    r.wait("gpsimd", "veP")
                    r.dma("gpsimd", out_e[128 * t:128 * (t + 1), :], s_d1[:],
                          "gpD")
            elif dbg == "rs":
                r.wait("sync", "cc")
                for t in range(8):
                    r.wait("sync", "veP")
                    r.dma("sync", s_d1[:], rs_out[128 * t:128 * (t + 1), :],
                          "syD")
                    r.wait("gpsimd", "syD")
                    r.dma("gpsimd", out_e[128 * t:128 * (t + 1), :], s_d1[:],
                          "gpD")
            elif dbg == "ut":
                s_ub2 = sb("s_ub2", (128, 1024), BF16)
                for t in range(8):
                    r.wait("sync", "veP")
                    r.dma("sync", s_ub2[:],
                          uT_full[1024 * t:1024 * t + 128, :], "syD")
                    r.wait("vector", "syD")
                    r.wait("vector", "gpD")
                    r.do("vector", lambda E: E.tensor_copy(
                        s_d1[:, 0:1024], s_ub2[:]), "veP")
                    r.wait("gpsimd", "veP")
                    r.dma("gpsimd", out_e[128 * t:128 * (t + 1), :], s_d1[:],
                          "gpD")
            else:
                buf, bdt, wid = src_map[dbg]
                s_db = sb("s_db", (128, DD), bdt)
                for t in range(8):
                    r.wait("sync", "veP")
                    r.dma("sync", s_db[:], buf[t][:, 0:DD], "syD")
                    r.wait("vector", "syD")
                    r.wait("vector", "gpD")
                    r.do("vector", lambda E: E.tensor_copy(s_d1[:], s_db[:]),
                         "veP")
                    r.wait("gpsimd", "veP")
                    r.dma("gpsimd", out_e[128 * t:128 * (t + 1), :], s_d1[:],
                          "gpD")
            r.wait("sync", "gpD")
            r.flush("DBG")
        top.close()
        return nc

    with ExitStack() as ph:
        sb = lambda n, s, d: ph.enter_context(nc.sbuf_tensor(n, list(s), d))
        s_acc = [sb(f"s_fa{t}", (128, DD), F32) for t in range(8)]
        s_f2 = sb("s_f2", (128, DD), F32)
        s_f3 = sb("s_f3", (128, DD), F32)
        for t in range(8):
            if t > 0:
                r.wait("sync", "veP")
            r.dma("sync", s_acc[t][:], x_own[128 * t:128 * (t + 1), :], "syD")
            r.dma("sync", s_f2[:], xn_sc[t], "syD")
            r.wait("vector", "syD")
            r.do("vector", lambda E, t=t: E.tensor_add(
                s_acc[t][:], s_acc[t][:], s_f2[:]), "veP")
            r.wait("sync", "veP")
            r.dma("sync", s_f2[:], ffn_sc[t], "syD")
            r.wait("vector", "syD")
            r.do("vector", lambda E, t=t: E.tensor_add(
                s_acc[t][:], s_acc[t][:], s_f2[:]), "veP")
        r.wait("sync", "cc")
        for t in range(8):
            if t > 0:
                r.wait("sync", "veP")
            r.dma("sync", s_f3[:], rs_out[128 * t:128 * (t + 1), :], "syD")
            r.wait("vector", "syD")
            if t > 0:
                r.wait("vector", "gpD")
            r.do("vector", lambda E, t=t: E.tensor_add(
                s_acc[t][:], s_acc[t][:], s_f3[:]), "veP")
            r.wait("gpsimd", "veP")
            r.dma("gpsimd", out_e[128 * t:128 * (t + 1), :], s_acc[t][:],
                  "gpD")
        r.wait("sync", "gpD")
        r.flush("FIN")

    top.close()
    return nc


def kernel(**inp):
    x = np.ascontiguousarray(inp["L0_attn_x"])
    norm_w = inp["L0_attn_norm_w"]
    W_in = inp["L0_attn_W_in"]
    conv_w = inp["L0_attn_conv_w"]
    conv_b = inp["L0_attn_conv_b"]
    W_x = inp["L0_attn_W_x"]
    W_dt = inp["L0_attn_W_dt"]
    b_dt = inp["L0_attn_b_dt"]
    A = inp["L0_attn_A"]
    D = inp["L0_attn_D"]
    W_out = inp["L0_attn_W_out"]
    lut = inp["L0_ffn_lut"]
    g_idx = inp["L0_ffn_gate_idx"]
    u_idx = inp["L0_ffn_up_idx"]
    d_idx = inp["L0_ffn_down_idx"]

    f32 = np.float32

    def gu_slabs(idx):
        t = idx.reshape(OT, 128, 16, 128)
        return t.transpose(0, 3, 2, 1).reshape(OT, 128, SLABW).astype(np.int16)

    def dn_slabs(idx):
        return np.ascontiguousarray(idx.T).reshape(OT, 128, DD).astype(np.int16)

    slabs = np.zeros((NSLAB, 128, SLABW), np.int16)
    slabs[0:43] = gu_slabs(g_idx)
    slabs[43:86] = gu_slabs(u_idx)
    slabs[86:129] = dn_slabs(d_idx)

    nw16 = np.ascontiguousarray(norm_w.reshape(16, 128).T.astype(f32))
    lut2 = np.ascontiguousarray(lut.reshape(1, NS).astype(f32))

    in_maps = []
    for k in range(NCORE):
        b, q = divmod(k, 4)
        rows = np.r_[1024 * q:1024 * (q + 1),
                     ED + 1024 * q:ED + 1024 * (q + 1)]
        Wr = W_in[rows]
        winTa = np.ascontiguousarray(
            Wr.reshape(16, 128, 16, 128).transpose(0, 3, 2, 1)
            .reshape(16, 128, DD).astype(f32))
        ch = slice(1024 * q, 1024 * (q + 1))
        wxTa = np.ascontiguousarray(
            W_x[:, ch].T.reshape(8, 128, 160).transpose(1, 0, 2)
            .reshape(128, 8 * 160).astype(f32))
        in_maps.append(dict(
            x_own=np.ascontiguousarray(x[b, 1024 * q:1024 * (q + 1)]),
            winT=winTa, nw16=nw16, wxT=wxTa,
            wdtT=np.ascontiguousarray(W_dt[ch].T.astype(f32)),
            bdt8=np.ascontiguousarray(b_dt[ch].reshape(8, 128).T.astype(f32)),
            a8=np.ascontiguousarray(
                A[ch].reshape(8, 128, NS).transpose(1, 0, 2)
                .reshape(128, 128).astype(f32)),
            d8=np.ascontiguousarray(D[ch].reshape(8, 128).T.astype(f32)),
            cw8=np.ascontiguousarray(
                conv_w[ch, 0, :].reshape(8, 128, 4).transpose(1, 0, 2)
                .reshape(128, 32).astype(f32)),
            cb8=np.ascontiguousarray(
                conv_b[ch].reshape(8, 128).T.astype(f32)),
            woT=np.ascontiguousarray(
                W_out[:, ch].T.reshape(8, 128, DD).astype(f32)),
            idxs=np.ascontiguousarray(slabs[SHARD * k:SHARD * (k + 1)]),
            lut_i=lut2,
            bm_i=np.ascontiguousarray(
                np.stack([np.full(128, 1.0 - b, np.float32),
                          np.full(128, float(b), np.float32)], axis=1)),
        ))

    nc = build_nc()
    res = run_bass_kernel_spmd(nc, in_maps, core_ids=list(range(NCORE)))
    out = np.empty((BB, LL, DD), np.float32)
    for k in range(NCORE):
        b, q = divmod(k, 4)
        out[b, 1024 * q:1024 * (q + 1)] = np.asarray(
            res.results[k]["out"], np.float32)
    return out
